# version 36
# baseline (speedup 1.0000x reference)
import contextlib
import numpy as np
import ml_dtypes
import concourse.bass as bass
import concourse.mybir as mybir
from concourse.bass_utils import run_bass_kernel_spmd

F32 = mybir.dt.float32
BF16 = mybir.dt.bfloat16
I32 = mybir.dt.int32
AF = mybir.ActivationFunctionType
ALU = mybir.AluOpType
AX = mybir.AxisListType

D, T, H = 1024, 2048, 3072
NCH, NHID, NTI = 8, 24, 16
QSCALE = float(D) ** -0.5
FSCALE = 1.0 / (127.0 * 127.0)
EPS_LN = 1e-5
EPS_A = 1e-8

R0 = 0
R1 = 16384
R2 = 32768
A_W0 = 49152
A_W1 = 50176
A_V0 = 51200
A_V1 = 52224
A_WD0 = 53248
A_WD1 = 56320
A_S1 = 59392
A_S1b = 61440
A_S2 = 63488
A_S2b = 65536
A_T0 = 67584
A_T1 = 68608
A_X0 = 69632
A_SQ0 = 70656
BP_COLS = 71680

F_XS0, F_XS1 = 0, 4096
F_ST0, F_ST1 = 8192, 10240
F_TMP = 12288
F_SGM, F_A, F_HS, F_BNEG, F_SNEG = 0, 2048, 4096, 6144, 8192
F_XCm0, F_XCm1 = 10240, 12288
F_Om = 14336
F_XC0, F_XC1, F_UT0, F_UT1, F_O0, F_O1 = 0, 2048, 4096, 6144, 8192, 10240
F_MISC = 16384
F_COLS = 16384 + 64


class Sem:
    __slots__ = ("h", "n", "name")

    def __init__(self, h, name):
        self.h = h
        self.n = 0
        self.name = name


def build_program(n_layers=2):
    nc = bass.Bass("TRN2", target_bir_lowering=False)
    te, ve, se, sy, gp = nc.tensor, nc.vector, nc.scalar, nc.sync, nc.gpsimd
    sc = nc.scalar

    xt = nc.dram_tensor("xt", [D, T], F32, kind="ExternalInput").ap()
    wz = nc.dram_tensor("wz", [NCH, 128, D], F32, kind="ExternalInput").ap()
    wh = nc.dram_tensor("wh", [NCH, 128, D], F32, kind="ExternalInput").ap()
    bzd = nc.dram_tensor("bzd", [128, NCH], F32, kind="ExternalInput").ap()
    bhd = nc.dram_tensor("bhd", [128, NCH], F32, kind="ExternalInput").ap()
    wg = nc.dram_tensor("wg", [NHID, 128, D], I32, kind="ExternalInput").ap()
    wu = nc.dram_tensor("wu", [NHID, 128, D], I32, kind="ExternalInput").ap()
    wd = nc.dram_tensor("wd", [NCH, 128, H], I32, kind="ExternalInput").ap()
    wqkv = nc.dram_tensor("wqkv", [3 * NCH, 128, D], F32, kind="ExternalInput").ap()
    wr = nc.dram_tensor("wr", [NCH, 128, D], F32, kind="ExternalInput").ap()
    wk = nc.dram_tensor("wk", [NCH, 128, D], F32, kind="ExternalInput").ap()
    wv = nc.dram_tensor("wv", [NCH, 128, D], F32, kind="ExternalInput").ap()
    murd = nc.dram_tensor("murd", [128, NCH], F32, kind="ExternalInput").ap()
    mukd = nc.dram_tensor("mukd", [128, NCH], F32, kind="ExternalInput").ap()
    onesd = nc.dram_tensor("onesd", [128, 128], BF16, kind="ExternalInput").ap()
    identd = nc.dram_tensor("identd", [128, 128], BF16, kind="ExternalInput").ap()
    out = nc.dram_tensor("out", [D, T], F32, kind="ExternalOutput").ap()
    x1s = nc.dram_tensor("x1s", [NCH, 128, T], F32, kind="Internal").ap()
    gs = nc.dram_tensor("gs", [NCH, 128, T], BF16, kind="Internal").ap()
    xnbs = nc.dram_tensor("xnbs", [NCH, 128, T], BF16, kind="Internal").ap()
    qs = nc.dram_tensor("qs", [NCH, 128, T], BF16, kind="Internal").ap()

    es = contextlib.ExitStack()
    BP = es.enter_context(nc.sbuf_tensor("BP", [128, BP_COLS], BF16))
    FP = es.enter_context(nc.sbuf_tensor("FP", [128, F_COLS], F32))
    ones = es.enter_context(nc.sbuf_tensor("ones", [128, 128], BF16))
    ident = es.enter_context(nc.sbuf_tensor("ident", [128, 128], BF16))
    CB = es.enter_context(nc.sbuf_tensor("CB", [128, 44], F32))
    psf = es.enter_context(nc.psum_tensor("psf", [128, 3072], F32))
    psb = es.enter_context(nc.psum_tensor("psb", [128, 2048], BF16))

    def mksem(name):
        return Sem(es.enter_context(nc.semaphore(name)), name)

    s_t = mksem("s_t")
    s_v = mksem("s_v")
    s_s = mksem("s_s")
    s_c = mksem("s_c")
    s_ln = [mksem("s_ln0"), mksem("s_ln1")]
    s_w = [mksem("s_w0"), mksem("s_w1")]
    s_wd = [mksem("s_wd0"), mksem("s_wd1")]
    s_go = [mksem("s_go0"), mksem("s_go1")]
    s_gi = [mksem("s_gi0"), mksem("s_gi1")]
    s_xc = [mksem("s_xc0"), mksem("s_xc1")]
    s_x1 = [mksem("s_x10"), mksem("s_x11")]
    s_gp = mksem("s_gp")
    s_no = mksem("s_no")
    s_sp = [mksem(f"s_sp{i}") for i in range(4)]
    s_ni = [mksem("s_ni0"), mksem("s_ni1")]
    s_qo = [mksem("s_qo0"), mksem("s_qo1")]
    s_qi = [mksem("s_qi0"), mksem("s_qi1")]
    s_ux = [mksem("s_ux0"), mksem("s_ux1")]
    s_ug = [mksem("s_ug0"), mksem("s_ug1")]
    s_uo = [mksem("s_uo0"), mksem("s_uo1")]
    s_ou = [mksem("s_ou0"), mksem("s_ou1")]
    s_qw = mksem("s_qw")

    def inc(instr, sem, k=1):
        instr.then_inc(sem.h, k)
        sem.n += k

    def wt(eng, sem, val=None):
        v = sem.n if val is None else val
        assert v <= sem.n, f"wait {sem.name} {v} > {sem.n}"
        if v > 0:
            eng.wait_ge(sem.h, v)

    class EngProxy:
        def __init__(self, eng, sem):
            self._eng = eng
            self._sem = sem

        def __getattr__(self, name):
            fn = getattr(self._eng, name)
            if name in ("dma_start", "dma_start_transpose", "wait_ge"):
                return fn

            def wrapped(*a, **k):
                wt(self._eng, self._sem)
                ins = fn(*a, **k)
                inc(ins, self._sem)
                return ins
            return wrapped

    vx = EngProxy(ve, s_v)
    sx = EngProxy(se, s_s)
    gx = EngProxy(gp, s_gp)

    with es:
        inc(sy.dma_start(ones[:], onesd), s_c, 16)
        inc(sy.dma_start(ident[:], identd), s_c, 16)
        inc(sy.dma_start(CB[:, 32:40], bzd), s_c, 16)
        inc(sy.dma_start(CB[:, 8:16], bhd), s_c, 16)
        inc(sy.dma_start(CB[:, 16:24], murd), s_c, 16)
        inc(sy.dma_start(CB[:, 24:32], mukd), s_c, 16)
        wt(ve, s_c)
        vx.memset(CB[:, 40:41], EPS_LN)
        vx.tensor_scalar(CB[:, 0:8], CB[:, 32:40], -1.0, None, ALU.mult)
        wt(se, s_v)
        wt(te, s_c)

        def ln_build(layer, mix_hook=None):
            st = {"aff": {}, "statfree": {}, "castuse": {}, "stslot": {},
                  "spill": {}}
            src_x1 = (layer == 1)
            entry_v, entry_s = s_v.n, s_s.n
            wt(te, s_v)
            wt(te, s_s)
            psx = psf[:, 2048:2560]
            psq = psf[:, 2560:3072]

            def LD(sp):
                par = sp % 2
                xsb = F_XS0 if par == 0 else F_XS1
                if sp >= 2:
                    wt(sy, s_v, st["aff"][sp - 2])
                else:
                    wt(sy, s_v, entry_v)
                    wt(sy, s_s, entry_s)
                if src_x1:
                    wt(sy, s_x1[0])
                    wt(sy, s_x1[1])
                for c in range(NCH):
                    srcap = (x1s[c, :, sp * 512:(sp + 1) * 512] if src_x1 else
                             xt[c * 128:(c + 1) * 128, sp * 512:(sp + 1) * 512])
                    ins = sy.dma_start(FP[:, xsb + c * 512: xsb + (c + 1) * 512], srcap)
                    inc(ins, s_ln[par], 16)

            def CSMM(sp):
                par = sp % 2
                xsb = F_XS0 if par == 0 else F_XS1
                xb0f = BP[:, A_S1: A_S1 + 4096]
                sqf = BP[:, A_S2: A_S2 + 4096]
                xsf = FP[:, xsb: xsb + 4096]
                wt(ve, s_ln[par])
                wt(ve, s_t, st["castuse"].get(sp - 1, 0))
                vx.tensor_copy(xb0f, xsf)
                wt(se, s_ln[par])
                wt(se, s_t, st["castuse"].get(sp - 1, 0))
                sx.activation(sqf, xsf, AF.Square)
                wt(te, s_s)
                wt(te, s_v)
                if sp >= 1:
                    wt(te, s_v, st["statfree"][sp - 1])
                for c in range(NCH):
                    te.matmul(psx, ones[:], xb0f[:, c * 512:(c + 1) * 512],
                              start=(c == 0), stop=(c == NCH - 1))
                    ins = te.matmul(psq, ones[:], sqf[:, c * 512:(c + 1) * 512],
                                    start=(c == 0), stop=(c == NCH - 1))
                inc(ins, s_t)
                st["castuse"][sp] = s_t.n

            def EP(sp):
                par = sp % 2
                stb = F_ST0 if par == 0 else F_ST1
                mub = FP[:, stb: stb + 512]
                m2 = FP[:, stb + 512: stb + 1024]
                varb = FP[:, stb + 1024: stb + 1536]
                rstd = FP[:, stb + 1536: stb + 2048]
                wt(ve, s_t)
                if sp >= 2:
                    wt(ve, s_s, st["stslot"][sp - 2])
                vx.tensor_scalar(mub, psx, 1.0 / D, None, ALU.mult)
                if src_x1:
                    vx.tensor_mul(m2, mub, mub)
                else:
                    wt(se, s_v)
                    if sp >= 2:
                        wt(se, s_v, st["aff"][sp - 2])
                    sx.activation(m2, mub, AF.Square)
                    wt(ve, s_s)
                vx.scalar_tensor_tensor(varb, psq, 1.0 / D, m2,
                                        ALU.mult, ALU.subtract)
                st["statfree"][sp] = s_v.n
                wt(se, s_v)
                sx.activation(varb, varb, AF.Ln, bias=CB[:, 40:41])
                sx.activation(rstd, varb, AF.Exp, scale=-0.5)
                st["stslot"][sp] = s_s.n

            def AFN(sp):
                par = sp % 2
                xsb = F_XS0 if par == 0 else F_XS1
                stb = F_ST0 if par == 0 else F_ST1
                mub = FP[:, stb: stb + 512]
                rstd = FP[:, stb + 1536: stb + 2048]
                wt(ve, s_s, st["stslot"][sp])
                xs_full = FP[:, xsb: xsb + 4096].rearrange(
                    "p (c j) -> p c j", c=8, j=512)
                tmpf = FP[:, 12288: 12288 + 4096].rearrange(
                    "p (c j) -> p c j", c=8, j=512)
                mub_b = bass.broadcast_tensor_aps(
                    xs_full, mub.rearrange("p (c j) -> p c j", c=1, j=512))[1]
                rstd_b = bass.broadcast_tensor_aps(
                    xs_full, rstd.rearrange("p (c j) -> p c j", c=1, j=512))[1]
                dstf = BP[:, R0: R0 + 8 * T].rearrange(
                    "p (c j) -> p c j", c=8, j=T)[:, :, sp * 512:(sp + 1) * 512]
                vx.tensor_sub(tmpf, xs_full, mub_b)
                vx.tensor_mul(dstf, tmpf, rstd_b)
                st["aff"][sp] = s_v.n
                if src_x1:
                    wt(sy, s_v)
                    for c in range(NCH):
                        inc(sy.dma_start(
                            xnbs[c][:, sp * 512:(sp + 1) * 512],
                            BP[:, R0 + c * T + sp * 512:
                               R0 + c * T + (sp + 1) * 512]), s_sp[sp], 16)

            stages = [lambda: LD(0), lambda: LD(1), lambda: CSMM(0), lambda: EP(0),
                      lambda: CSMM(1), lambda: AFN(0), lambda: LD(2), lambda: EP(1),
                      lambda: AFN(1), lambda: LD(3), lambda: CSMM(2), lambda: EP(2),
                      lambda: AFN(2), lambda: CSMM(3), lambda: EP(3), lambda: AFN(3)]
            return {"stages": stages, "st": st}

        ln0 = ln_build(0)
        for f in ln0["stages"][:10]:
            f()
        ln0_suffix = ln0["stages"][10:]

        wfree = {0: 0, 1: 0}
        dfree = {0: 0, 1: 0}

        def load_tiles(entries, par, sem, gate):
            wt(gp, s_t, gate)
            for dram3, idx, col, w in entries:
                inc(gp.dma_start(BP[:, col: col + w], dram3[idx]), sem, 16)

        wt(te, s_v, ln0["st"]["aff"][0])
        q_free = [0, 0, 0, 0]
        ffn_q_gate = {1: ln0["st"]["aff"][1]}
        hsil_free = [0, 0]
        dn_free = [0, 0, 0]
        go_mark = [0, 0]
        half_gdone = 0
        def pf_gu(m, par):
            wslot = A_W0 if par == 0 else A_W1
            vslot = A_V0 if par == 0 else A_V1
            load_tiles([(wg, m, wslot, D), (wu, m, vslot, D)], par, s_w[par],
                       wfree[par])

        def pf_dn(o, par):
            dslot = A_WD0 if par == 0 else A_WD1
            load_tiles([(wd, o, dslot, H)], par, s_wd[par], dfree[par])


        def gcol(e):
            if e < 4:
                return R2 + 8192 + e * 2048
            if e < 6:
                return A_S1 if e == 4 else A_S1b
            return A_S2 if e == 6 else A_S2b

        gsl0_mark = {}
        gsl1_mark = {}
        ln0_sched = {3: 0, 5: 1, 7: 2, 9: 3, 11: 4, 13: 5}
        for half in range(2):
            t0 = half * 1024
            if half == 1:
                wt(te, s_s, half_gdone)
                wt(te, s_v, ln0["st"]["aff"][3])
            pf_gu(0, 0)
            pf_gu(1, 1)
            pf_dn(0, 0)
            pf_dn(1, 1)
            for m in range(NHID):
                if half == 0 and m in ln0_sched:
                    ln0_suffix[ln0_sched[m]]()
                par = m % 2
                wslot = A_W0 if par == 0 else A_W1
                vslot = A_V0 if par == 0 else A_V1
                wt(te, s_w[par])
                for sp in range(2):
                    qg, qu = sp * 2, sp * 2 + 1
                    pg = psf[:, qg * 512:(qg + 1) * 512]
                    pu = psf[:, qu * 512:(qu + 1) * 512]
                    if half == 0 and m == 0 and sp == 1:
                        wt(te, s_v, ffn_q_gate[1])
                    wt(te, s_v, q_free[qg])
                    for k in range(NCH):
                        lhsT = BP[:, wslot + k * 128: wslot + (k + 1) * 128]
                        rhs = BP[:, R0 + k * T + t0 + sp * 512:
                                 R0 + k * T + t0 + sp * 512 + 512]
                        te.matmul(pg, lhsT, rhs, start=(k == 0), stop=(k == NCH - 1))
                    wt(te, s_v, q_free[qu])
                    for k in range(NCH):
                        lhsT = BP[:, vslot + k * 128: vslot + (k + 1) * 128]
                        rhs = BP[:, R0 + k * T + t0 + sp * 512:
                                 R0 + k * T + t0 + sp * 512 + 512]
                        ins = te.matmul(pu, lhsT, rhs, start=(k == 0), stop=(k == NCH - 1))
                    inc(ins, s_t)
                    hsil = (BP[:, A_T0 + sp * 512: A_T0 + sp * 512 + 512] if par == 0
                            else BP[:, A_T1 + sp * 512: A_T1 + sp * 512 + 512])
                    wt(se, s_t)
                    wt(se, s_v, hsil_free[par])
                    sx.activation(hsil, pg, AF.Silu, scale=1.0 / 127.0)
                    wt(ve, s_s)
                    hcol = (R1 + m * 1024) if m < 16 else (R2 + (m - 16) * 1024)
                    hdst = BP[:, hcol + sp * 512: hcol + sp * 512 + 512]
                    vx.tensor_mul(hdst, hsil, pu)
                    q_free[qg] = s_v.n
                    q_free[qu] = s_v.n
                hsil_free[par] = s_v.n
                wfree[par] = s_t.n
                if m + 2 < NHID:
                    pf_gu(m + 2, par)
            hdone = s_v.n
            wt(te, s_v, ln0["st"]["statfree"][3])
            if half == 0:
                for o in range(NCH):
                    par = o % 2
                    dslot = A_WD0 if par == 0 else A_WD1
                    wt(te, s_wd[par])
                    wt(te, s_v, hdone)
                    u = o % 3
                    pd = psf[:, u * 1024:(u + 1) * 1024]
                    wt(te, s_s, dn_free[u])
                    for sp in range(2):
                        for m in range(NHID):
                            lhsT = BP[:, dslot + m * 128: dslot + (m + 1) * 128]
                            hcol = (R1 + m * 1024) if m < 16 else (R2 + (m - 16) * 1024)
                            rhs = BP[:, hcol + sp * 512: hcol + sp * 512 + 512]
                            ins = te.matmul(pd[:, sp * 512: sp * 512 + 512], lhsT, rhs,
                                            start=(m == 0), stop=(m == NHID - 1))
                    inc(ins, s_t)
                    dfree[par] = s_t.n
                    if o + 2 < NCH:
                        pf_dn(o + 2, par)
                    gdst = BP[:, gcol(o): gcol(o) + 1024]
                    wt(se, s_t)
                    sx.activation(gdst, pd, AF.Silu, scale=FSCALE)
                    dn_free[u] = s_s.n
                    gsl0_mark[o] = s_s.n
                half_gdone = s_s.n
                continue
            MF_SGM = [0, 2048]
            MF_BNEG = [4096, 6144]
            MF_SNEG = [8192, 10240]
            MF_XCM = [12288, 14336]
            entry_v = s_v.n
            scanmark = [entry_v, entry_v]
            xcmark = [0, 0]
            addmark = [0, 0]
            dn1_free = s_s.n
            mu_free = [s_s.n, s_s.n]

            def pf_mxw(e, par):
                wslot = A_W0 if par == 0 else A_W1
                vslot = A_V0 if par == 0 else A_V1
                load_tiles([(wz, e, wslot, D), (wh, e, vslot, D)], par, s_w[par],
                           wfree[par])

            def pf_x(e, par):
                xcs_ = FP[:, MF_XCM[par]: MF_XCM[par] + T]
                wt(sy, s_v, xcmark[par])
                wt(sy, s_v, entry_v if e < 2 else 0)
                inc(sy.dma_start(xcs_, xt[e * 128:(e + 1) * 128, :]), s_xc[par], 16)

            pf_mxw(0, 0)
            pf_mxw(1, 1)
            pf_x(0, 0)
            pf_x(1, 1)
            for o in range(NCH):
                par = o % 2
                dslot = A_WD0 if par == 0 else A_WD1
                wt(te, s_wd[par])
                wt(te, s_v, hdone)
                pd = psf[:, 0:1024]
                wt(te, s_s, dn1_free)
                for sp in range(2):
                    for m in range(NHID):
                        lhsT = BP[:, dslot + m * 128: dslot + (m + 1) * 128]
                        hcol = (R1 + m * 1024) if m < 16 else (R2 + (m - 16) * 1024)
                        rhs = BP[:, hcol + sp * 512: hcol + sp * 512 + 512]
                        ins = te.matmul(pd[:, sp * 512: sp * 512 + 512], lhsT, rhs,
                                        start=(m == 0), stop=(m == NHID - 1))
                inc(ins, s_t)
                dfree[par] = s_t.n
                if o + 2 < NCH:
                    pf_dn(o + 2, par)
                gdst = BP[:, gcol(o) + 1024: gcol(o) + 2048]
                wt(se, s_t)
                sx.activation(gdst, pd, AF.Silu, scale=FSCALE)
                dn1_free = s_s.n
                gsl1_mark[o] = s_s.n
                e = o
                wslot = A_W0 if par == 0 else A_W1
                vslot = A_V0 if par == 0 else A_V1
                sgm = FP[:, MF_SGM[par]: MF_SGM[par] + T]
                bneg = FP[:, MF_BNEG[par]: MF_BNEG[par] + T]
                sneg = FP[:, MF_SNEG[par]: MF_SNEG[par] + T]
                xcs = FP[:, MF_XCM[par]: MF_XCM[par] + T]
                wt(te, s_w[par])
                for j in range(4):
                    mslot = wslot if (j % 2 == 0) else vslot
                    hh = j // 2
                    u2 = (e * 4 + j) % 2
                    pu_ap = psf[:, 1024 + u2 * 1024: 2048 + u2 * 1024]
                    wt(te, s_s, mu_free[u2])
                    for k in range(NCH):
                        lhsT = BP[:, mslot + k * 128: mslot + (k + 1) * 128]
                        for sps in range(2):
                            rhs = BP[:, R0 + k * T + hh * 1024 + sps * 512:
                                     R0 + k * T + hh * 1024 + sps * 512 + 512]
                            ins = te.matmul(pu_ap[:, sps * 512: sps * 512 + 512],
                                            lhsT, rhs,
                                            start=(k == 0), stop=(k == NCH - 1))
                    inc(ins, s_t)
                    wt(se, s_t)
                    wt(se, s_v, scanmark[par])
                    if j % 2 == 0:
                        dst = FP[:, MF_SGM[par] + hh * 1024:
                                 MF_SGM[par] + (hh + 1) * 1024]
                        sx.activation(dst, pu_ap, AF.Sigmoid, scale=-1.0,
                                      bias=CB[:, e:e + 1])
                    else:
                        dst = FP[:, MF_BNEG[par] + hh * 1024:
                                 MF_BNEG[par] + (hh + 1) * 1024]
                        sx.activation(dst, pu_ap, AF.Identity, bias=CB[:, 8 + e:9 + e])
                    mu_free[u2] = s_s.n
                evac_done = s_s.n
                wt(ve, s_s, evac_done)
                vx.scalar_tensor_tensor(bneg, sgm, 1.0, bneg, ALU.subtract, ALU.mult)
                vx.tensor_scalar(sgm, sgm, EPS_A, None, ALU.add)
                wt(ve, s_x1[par])
                vx.tensor_tensor_scan(sneg, sgm, bneg, 0.0, ALU.mult, ALU.add)
                scanmark[par] = s_v.n
                gin = BP[:, gcol(e): gcol(e) + T]
                wt(ve, s_s, gsl1_mark[e])
                vx.scalar_tensor_tensor(sneg, sneg, -1.0, gin, ALU.mult, ALU.mult)
                wt(ve, s_xc[par])
                vx.tensor_add(sneg, sneg, xcs)
                xcmark[par] = s_v.n
                addmark[par] = s_v.n
                wt(sy, s_v, addmark[par])
                inc(sy.dma_start(x1s[e], sneg), s_x1[par], 16)
                wfree[par] = s_t.n
                if e + 2 < NCH:
                    pf_mxw(e + 2, par)
                    pf_x(e + 2, par)

        if n_layers == 1:
            wt(sy, s_x1[0])
            wt(sy, s_x1[1])
            for c in range(NCH):
                inc(sy.dma_start(out[c * 128:(c + 1) * 128, :], x1s[c]),
                    s_ou[c % 2], 16)
            wt(sy, s_ou[0])
            wt(sy, s_ou[1])
            return nc

        def mix_span(sp, c):
            a = sp * 512
            xnc = BP[:, R0 + c * T: R0 + (c + 1) * T]
            xr_c = BP[:, R1 + c * T: R1 + (c + 1) * T]
            xk_c = BP[:, R2 + c * T: R2 + (c + 1) * T]
            dtmp = BP[:, A_X0: A_X0 + 512]
            if sp == 0:
                vx.tensor_sub(dtmp[:, 1:512], xnc[:, 1:512], xnc[:, 0:511])
                vx.scalar_tensor_tensor(xr_c[:, 1:512], dtmp[:, 1:512],
                                        CB[:, 16 + c:17 + c], xnc[:, 0:511],
                                        ALU.mult, ALU.add)
                vx.tensor_scalar(xr_c[:, 0:1], xnc[:, 0:1], CB[:, 16 + c:17 + c],
                                 None, ALU.mult)
                vx.scalar_tensor_tensor(xk_c[:, 1:512], dtmp[:, 1:512],
                                        CB[:, 24 + c:25 + c], xnc[:, 0:511],
                                        ALU.mult, ALU.add)
                vx.tensor_scalar(xk_c[:, 0:1], xnc[:, 0:1], CB[:, 24 + c:25 + c],
                                 None, ALU.mult)
            elif sp == 2:
                bc = BP[:, A_X0 + 512 + c: A_X0 + 513 + c]
                prev = xnc[:, a: a + 511]
                vx.tensor_sub(dtmp[:, 0:1], xnc[:, a: a + 1], bc)
                vx.tensor_sub(dtmp[:, 1:512], xnc[:, a + 1: a + 512], prev)
                vx.scalar_tensor_tensor(xr_c[:, a: a + 1], dtmp[:, 0:1],
                                        CB[:, 16 + c:17 + c], bc,
                                        ALU.mult, ALU.add)
                vx.scalar_tensor_tensor(xr_c[:, a + 1: a + 512], dtmp[:, 1:512],
                                        CB[:, 16 + c:17 + c], prev,
                                        ALU.mult, ALU.add)
                vx.scalar_tensor_tensor(xk_c[:, a: a + 1], dtmp[:, 0:1],
                                        CB[:, 24 + c:25 + c], bc,
                                        ALU.mult, ALU.add)
                vx.scalar_tensor_tensor(xk_c[:, a + 1: a + 512], dtmp[:, 1:512],
                                        CB[:, 24 + c:25 + c], prev,
                                        ALU.mult, ALU.add)
            else:
                prev = xnc[:, a - 1: a + 511]
                vx.tensor_sub(dtmp, xnc[:, a: a + 512], prev)
                vx.scalar_tensor_tensor(xr_c[:, a: a + 512], dtmp,
                                        CB[:, 16 + c:17 + c], prev,
                                        ALU.mult, ALU.add)
                vx.scalar_tensor_tensor(xk_c[:, a: a + 512], dtmp,
                                        CB[:, 24 + c:25 + c], prev,
                                        ALU.mult, ALU.add)

        qw_cols = [A_WD0, A_WD0 + 1024, A_WD0 + 2048, A_WD1, A_WD1 + 1024,
                   A_WD1 + 2048, A_V0, A_V1]
        l0_t = s_t.n
        l0_v = s_v.n
        l0_go = [s_go[0].n, s_go[1].n]
        wt(gp, s_t, l0_t)
        for e in range(NCH):
            inc(gp.dma_start(BP[:, qw_cols[e]: qw_cols[e] + 1024], wqkv[e]),
                s_qw, 16)

        kseq = {0: [(0, 0), (0, 2), (0, 4), (0, 6), (1, 0), (1, 2), (1, 4),
                    (1, 6)],
                1: [(0, 1), (0, 3), (0, 5), (0, 7), (1, 1), (1, 3), (1, 5),
                    (1, 7)]}
        kload_i = {0: 0, 1: 0}

        def pf_wk2(par, gate):
            i = kload_i[par]
            if i < len(kseq[par]):
                _, o = kseq[par][i]
                load_tiles([(wk, o, A_W0 if par == 0 else A_W1, D)], par,
                           s_w[par], gate)
                kload_i[par] = i + 1

        pf_wk2(0, l0_t)
        pf_wk2(1, l0_t)

        ln1 = ln_build(1)
        st1 = ln1["st"]
        q_bank_free = [0, 0, 0, 0]
        qstg_free = [0, 0]
        mixm = {}

        def q_mm(sp, e):
            b = e % 4 if sp < 2 else e % 2
            pq = psf[:, b * 512: b * 512 + 512]
            wt(te, s_qw)
            wt(te, s_v, st1["aff"][sp])
            wt(te, s_s, q_bank_free[b])
            for c in range(NCH):
                lhsT = BP[:, qw_cols[e] + c * 128: qw_cols[e] + (c + 1) * 128]
                rhs = BP[:, R0 + c * T + sp * 512: R0 + c * T + sp * 512 + 512]
                ins = te.matmul(pq, lhsT, rhs, start=(c == 0), stop=(c == NCH - 1))
            inc(ins, s_t)

        def q_dr(sp, e):
            b = e % 4 if sp < 2 else e % 2
            pq = psf[:, b * 512: b * 512 + 512]
            par = e % 2
            stg = BP[:, (A_T0 if par == 0 else A_T1):][:, 0:512]
            wt(se, s_t)
            wt(se, s_v, l0_v)
            wt(se, s_go[par], l0_go[par])
            wt(se, s_qo[par], qstg_free[par])
            sx.activation(stg, pq, AF.Identity)
            q_bank_free[b] = s_s.n
            wt(sc, s_s)
            inc(sc.dma_start(qs[e, :, sp * 512:(sp + 1) * 512], stg),
                s_qo[par], 16)
            qstg_free[par] = s_qo[par].n

        def q_mma(sp):
            for e in range(4):
                q_mm(sp, e)

        def q_tail(sp):
            for e in range(4):
                q_dr(sp, e)
            for e in range(4, NCH):
                q_mm(sp, e)
            for e in range(4, NCH):
                q_dr(sp, e)

        def q_block(sp):
            for e in range(NCH):
                q_mm(sp, e)
                q_dr(sp, e)

        def mixes(sp):
            for c in range(NCH):
                mix_span(sp, c)
            mixm[sp] = s_v.n

        def bcol_save():
            src = BP[:, R0: R0 + 8 * T].rearrange(
                "p (c j) -> p c j", c=8, j=T)[:, :, 1023:1024]
            dst = BP[:, A_X0 + 512: A_X0 + 520].rearrange(
                "p (c j) -> p c j", c=8, j=1)
            vx.tensor_copy(dst, src)

        k_relu_mark = 0
        kact_done = 0

        def k_unit(hh, o):
            nonlocal k_relu_mark, kact_done
            par = o % 2
            wslot = A_W0 if par == 0 else A_W1
            pu_ap = psf[:, 1024:2048]
            wt(te, s_w[par])
            wt(te, s_v, mixm[1 if hh == 0 else 3])
            wt(te, s_s, k_relu_mark)
            if hh == 0 and o == 0:
                wt(te, s_s, q_bank_free[2])
                wt(te, s_s, q_bank_free[3])
            for k in range(NCH):
                lhsT = BP[:, wslot + k * 128: wslot + (k + 1) * 128]
                for sps in range(2):
                    rhs = BP[:, R2 + k * T + hh * 1024 + sps * 512:
                             R2 + k * T + hh * 1024 + sps * 512 + 512]
                    ins = te.matmul(pu_ap[:, sps * 512: sps * 512 + 512], lhsT,
                                    rhs, start=(k == 0), stop=(k == NCH - 1))
            inc(ins, s_t)
            kmm = s_t.n
            pf_wk2(par, kmm)
            rt = BP[:, (A_T0 if par == 0 else A_T1):][:, 512:1024]
            kdst = BP[:, R0 + o * T + hh * 1024: R0 + o * T + (hh + 1) * 1024]
            for half in range(2):
                wt(se, s_t, kmm)
                sx.activation(rt, pu_ap[:, half * 512:(half + 1) * 512],
                              AF.Relu)
                if half == 1:
                    k_relu_mark = s_s.n
                wt(se, s_v, mixm[1 if hh == 0 else 3])
                wt(se, s_sp[0 if hh == 0 else 2])
                wt(se, s_sp[1 if hh == 0 else 3])
                sx.activation(kdst[:, half * 512:(half + 1) * 512], rt,
                              AF.Square)
            kact_done = s_s.n

        def post11():
            q_tail(1)
            for o in range(4):
                k_unit(0, o)

        def post15():
            mixes(3)
            q_block(3)
            for o in range(NCH):
                k_unit(1, o)

        post = {5: lambda: (mixes(0), q_mma(0)), 7: lambda: q_tail(0),
                8: lambda: (mixes(1), bcol_save(), q_mma(1)), 11: post11,
                12: lambda: (mixes(2), q_block(2)),
                14: lambda: [k_unit(0, o) for o in range(4, NCH)],
                15: post15}
        for idx, f in enumerate(ln1["stages"]):
            f()
            if idx in post:
                post[idx]()
        mix_done = s_v.n
        wfree = {0: s_t.n, 1: s_t.n}

        wt(sc, s_t)
        for _i in range(4):
            wt(sc, s_sp[_i])
        for c in range(NCH):
            inc(sc.dma_start(BP[:, R2 + c * T: R2 + (c + 1) * T], xnbs[c]),
                s_ni[0], 16)

        wt(te, s_v)
        wt(te, s_s)
        wt(ve, s_qo[0])
        wt(ve, s_qo[1])
        kvfree = [0, 0]
        kvdone = [0, 0]
        rp_free = 0
        rt2_free = [mix_done, mix_done]
        ts_free = 0
        sg_free = [0, 0]
        go2_mark = [s_go[0].n, s_go[1].n]

        def pf_vr(o, par):
            load_tiles([(wv, o, A_W0 if par == 0 else A_W1, D),
                        (wr, o, A_V0 if par == 0 else A_V1, D)], par, s_w[par],
                       wfree[par])

        pf_vr(0, 0)
        pf_vr(1, 1)
        for o in range(NCH):
            par = o % 2
            wslot = A_W0 if par == 0 else A_W1
            vslot = A_V0 if par == 0 else A_V1
            wt(te, s_w[par])
            wt(te, s_s, kact_done)
            for piece in range(2):
                pv = psf[:, piece * 1024:(piece + 1) * 1024]
                wt(te, s_v, kvfree[piece])
                for k in range(NCH):
                    lhsT = BP[:, wslot + k * 128: wslot + (k + 1) * 128]
                    for sps in range(2):
                        rhs = BP[:, R0 + k * T + piece * 1024 + sps * 512:
                                 R0 + k * T + piece * 1024 + sps * 512 + 512]
                        ins = te.matmul(pv[:, sps * 512: sps * 512 + 512], lhsT, rhs,
                                        start=(k == 0), stop=(k == NCH - 1))
                inc(ins, s_t)
                kvdone[piece] = s_t.n
            for hh in range(2):
                pr = psf[:, 2048:3072]
                wt(te, s_s, rp_free)
                for k in range(NCH):
                    lhsT = BP[:, vslot + k * 128: vslot + (k + 1) * 128]
                    for sps in range(2):
                        rhs = BP[:, R1 + k * T + hh * 1024 + sps * 512:
                                 R1 + k * T + hh * 1024 + sps * 512 + 512]
                        ins = te.matmul(pr[:, sps * 512: sps * 512 + 512], lhsT, rhs,
                                        start=(k == 0), stop=(k == NCH - 1))
                inc(ins, s_t)
                rtmp = BP[:, (A_S2 if par == 0 else A_S2b):][:, hh * 1024:(hh + 1) * 1024]
                wt(se, s_t)
                wt(se, s_v, rt2_free[par])
                sx.activation(rtmp, pr, AF.Sigmoid)
                rp_free = s_s.n
                tslot = FP[:, F_TMP: F_TMP + 1024]
                wt(ve, s_s)
                wt(ve, s_t, kvdone[hh])
                wt(ve, s_s, ts_free)
                vx.tensor_mul(tslot, rtmp, psf[:, hh * 1024:(hh + 1) * 1024])
                rt2_free[par] = s_v.n
                kvfree[hh] = s_v.n
                sg = BP[:, (A_X0 if hh == 0 else A_SQ0):][:, 0:1024]
                wt(se, s_v)
                wt(se, s_v, sg_free[hh])
                sx.activation(sg, tslot, AF.Sigmoid)
                ts_free = s_s.n
                gtmp = BP[:, (A_T0 if par == 0 else A_T1):][:, 0:1024]
                wt(ve, s_s)
                wt(ve, s_go[par], go2_mark[par])
                vx.tensor_mul(gtmp, tslot, sg)
                sg_free[hh] = s_v.n
                wt(sc, s_v)
                inc(sc.dma_start(gs[o, :, hh * 1024:(hh + 1) * 1024], gtmp),
                    s_go[par], 16)
                go2_mark[par] = s_go[par].n
            wfree[par] = s_t.n
            if o + 2 < NCH:
                pf_vr(o + 2, par)
        g1_done = [s_go[0].n, s_go[1].n]
        rwkv_t_done = s_t.n

        wt(te, s_v)
        wt(te, s_s)
        u_free3 = [0, 0, 0]
        qtmp_free = [0, 0]
        vtp_free = {0: 0, 1: 0}

        def pf_qkv(e, par):
            load_tiles([(wqkv, e, A_W0 if par == 0 else A_W1, D)], par, s_w[par],
                       wfree[par])

        pf_qkv(NCH, 0)
        pf_qkv(NCH + 1, 1)
        for e in range(NCH, 3 * NCH):
            par = e % 2
            wslot = A_W0 if par == 0 else A_W1
            wt(te, s_w[par])
            wt(te, s_ni[0])
            vdefer = []
            for hh in range(2):
                u = (e * 2 + hh) % 3
                pu_ap = psf[:, u * 1024:(u + 1) * 1024]
                wt(te, s_v, u_free3[u])
                for k in range(NCH):
                    lhsT = BP[:, wslot + k * 128: wslot + (k + 1) * 128]
                    for sps in range(2):
                        rhs = BP[:, R2 + k * T + hh * 1024 + sps * 512:
                                 R2 + k * T + hh * 1024 + sps * 512 + 512]
                        ins = te.matmul(pu_ap[:, sps * 512: sps * 512 + 512], lhsT, rhs,
                                        start=(k == 0), stop=(k == NCH - 1))
                inc(ins, s_t)
                if e < NCH:
                    qtmp = BP[:, (A_S1 if par == 0 else A_S1b):][:, hh * 1024:(hh + 1) * 1024]
                    wt(ve, s_t)
                    wt(ve, s_qo[par], qtmp_free[par])
                    vx.tensor_copy(qtmp, pu_ap)
                    u_free3[u] = s_v.n
                    wt(sy, s_v)
                    inc(sy.dma_start(qs[e, :, hh * 1024:(hh + 1) * 1024], qtmp),
                        s_qo[par], 16)
                    if hh == 1:
                        qtmp_free[par] = s_qo[par].n
                elif e < 2 * NCH:
                    kdst = BP[:, R1 + (e - NCH) * T + hh * 1024:
                              R1 + (e - NCH) * T + (hh + 1) * 1024]
                    wt(ve, s_t)
                    vx.tensor_copy(kdst, pu_ap)
                    u_free3[u] = s_v.n
                else:
                    vtmp = BP[:, (A_S1 if par == 0 else A_S1b):][:, hh * 1024:(hh + 1) * 1024]
                    wt(ve, s_t)
                    wt(ve, s_qo[par], qtmp_free[par])
                    vx.tensor_copy(vtmp, pu_ap)
                    u_free3[u] = s_v.n
                    vdefer.append((hh, vtmp, s_v.n))
            for (hh, vtmp, vready) in vdefer:
                c = e - 2 * NCH
                wt(te, s_v, vready)
                wt(te, s_v, vtp_free[hh])
                for tt in range(8):
                    ins = te.transpose(
                        psb[:, hh * 1024 + tt * 128: hh * 1024 + (tt + 1) * 128],
                        vtmp[:, tt * 128:(tt + 1) * 128], ident[:])
                inc(ins, s_t)
                wt(ve, s_t)
                base = R0 + hh * 8192 + c * 128
                dst_ap = BP[:, base: base + 8192].rearrange(
                    "p (t j) -> p t j", t=8, j=1024)[:, :, 0:128]
                src_ap = psb[:, hh * 1024: hh * 1024 + 1024].rearrange(
                    "p (t j) -> p t j", t=8, j=128)
                vx.tensor_copy(dst_ap, src_ap)
                vtp_free[hh] = s_v.n
            wfree[par] = s_t.n
            if e + 2 < 3 * NCH:
                pf_qkv(e + 2, par)
        qkv_v_done = s_v.n

        wt(te, s_v)
        wt(te, s_s)
        wt(sy, s_go[0])
        wt(sy, s_go[1])
        exp_done = {0: 0, 1: 0}
        attb_free = {0: 0, 1: 0}
        mx_free = {0: 0, 1: 0}
        qt_free = {0: 0, 1: 0}
        psb_free = qkv_v_done
        pav_free = 0
        attT_free = {0: 0, 1: 0}
        sc_read = 0

        def pf_q(i, par):
            qt_ = BP[:, (A_T0 if par == 0 else A_T1):][:, 0:1024]
            wt(sy, s_qo[0])
            wt(sy, s_qo[1])
            wt(sy, s_t, qt_free[par])
            sap = qs[:, :, i * 128:(i + 1) * 128].rearrange("e p j -> p e j")
            inc(sy.dma_start(qt_.rearrange("p (e j) -> p e j", e=8, j=128), sap),
                s_qi[par], 16)

        pf_q(0, 0)
        pf_q(1, 1)
        tstate = {}

        tstate[("attT", -1)] = qkv_v_done

        def at_AB(i):
            nonlocal psb_free
            par = i % 2
            attb = BP[:, (A_S2 if par == 0 else A_S2b):][:, 0:T]
            wt(te, s_s, tstate[("exp", i)])
            wt(te, s_v, tstate[("attT", i - 1)])
            for tt in range(8, NTI):
                ins = te.transpose(psb[:, tt * 128:(tt + 1) * 128],
                                   attb[:, tt * 128:(tt + 1) * 128], ident[:])
            inc(ins, s_t)
            wt(te, s_v, psb_free)
            for tt in range(8):
                ins = te.transpose(psb[:, tt * 128:(tt + 1) * 128],
                                   attb[:, tt * 128:(tt + 1) * 128], ident[:])
            inc(ins, s_t)
            attb_free[par] = s_t.n
            attT = BP[:, (A_S1 if par == 0 else A_S1b):][:, 0:T]
            wt(ve, s_t)
            wt(ve, s_t, attT_free[par])
            vx.tensor_copy(attT, psb[:, 0:2048])
            tstate[("attT", i)] = s_v.n
            psb_free = s_v.n

        def at_CD(i):
            par = i % 2
            qt = BP[:, (A_T0 if par == 0 else A_T1):][:, 0:1024]
            psc = psf[:, 0:2048]
            wt(te, s_qi[par])
            wt(te, s_s, exp_done[0])
            wt(te, s_s, exp_done[1])
            wt(te, s_v, tstate.get(("rm", 0), 0))
            wt(te, s_v, tstate.get(("rm", 1), 0))
            for k in range(NCH):
                lhsT = qt[:, k * 128:(k + 1) * 128]
                for sps in range(4):
                    rhs = BP[:, R1 + k * T + sps * 512: R1 + k * T + sps * 512 + 512]
                    ins = te.matmul(psc[:, sps * 512: sps * 512 + 512], lhsT, rhs,
                                    start=(k == 0), stop=(k == NCH - 1))
            inc(ins, s_t)
            qt_free[par] = s_t.n
            mx = FP[:, F_MISC + par * 8: F_MISC + par * 8 + 1]
            bias = FP[:, F_MISC + par * 8 + 1: F_MISC + par * 8 + 2]
            wt(ve, s_t)
            wt(ve, s_s, mx_free[par])
            vx.reduce_max(mx, psc, axis=AX.X)
            vx.tensor_scalar(bias, mx, -QSCALE, None, ALU.mult)
            tstate[("rm", par)] = s_v.n

        def at_E(i):
            nonlocal pav_free
            par = i % 2
            attT = BP[:, (A_S1 if par == 0 else A_S1b):][:, 0:T]
            pav = psf[:, 2048:3072]
            wt(te, s_v, tstate[("attT", i)])
            wt(te, s_v, pav_free)
            for sps in range(2):
                for tt in range(NTI):
                    lhsT = attT[:, tt * 128:(tt + 1) * 128]
                    rhs = BP[:, R0 + tt * 1024 + sps * 512:
                             R0 + tt * 1024 + sps * 512 + 512]
                    ins = te.matmul(pav[:, sps * 512: sps * 512 + 512], lhsT, rhs,
                                    start=(tt == 0), stop=(tt == NTI - 1))
                inc(ins, s_t)
                if sps == 0:
                    tstate[("avh", i)] = s_t.n
            attT_free[par] = s_t.n
            tstate[("av", i)] = s_t.n

        def at_F(i):
            par = i % 2
            psc = psf[:, 0:2048]
            bias = FP[:, F_MISC + par * 8 + 1: F_MISC + par * 8 + 2]
            sums = FP[:, F_MISC + par * 8 + 2: F_MISC + par * 8 + 3]
            attb = BP[:, (A_S2 if par == 0 else A_S2b):][:, 0:T]
            wt(se, s_v, tstate[("rm", par)])
            wt(se, s_t, attb_free[par])
            sx.activation(attb, psc, AF.Exp, scale=QSCALE, bias=bias,
                          accum_out=sums)
            exp_done[par] = s_s.n
            mx_free[par] = s_s.n
            tstate[("exp", i)] = s_s.n

        def pf_upd_tile(i, par):
            xslot = FP[:, (F_XC0 if par == 0 else F_XC1):][:, 0:1024]
            wt(gp, s_v, tstate.get(("updx", par), 0))
            wt(gp, s_x1[0])
            wt(gp, s_x1[1])
            sap = x1s[:, :, i * 128:(i + 1) * 128].rearrange("e p j -> p e j")
            inc(gp.dma_start(xslot.rearrange("p (e j) -> p e j", e=8, j=128), sap),
                s_ux[par], 16)
            gslot = BP[:, (A_X0 if par == 0 else A_SQ0):][:, 0:1024]
            wt(gp, s_go[0], g1_done[0])
            wt(gp, s_go[1], g1_done[1])
            wt(gp, s_v, tstate.get(("updg", par), 0))
            gap_ = gs[:, :, i * 128:(i + 1) * 128].rearrange("e p j -> p e j")
            inc(gp.dma_start(gslot.rearrange("p (e j) -> p e j", e=8, j=128), gap_),
                s_ug[par], 16)

        def at_GHJ(i):
            nonlocal psb_free, pav_free
            par = i % 2
            qt = BP[:, (A_T0 if par == 0 else A_T1):][:, 0:1024]
            rec = FP[:, F_MISC + par * 8 + 3: F_MISC + par * 8 + 4]
            avs = BP[:, R2 + par * 1024: R2 + par * 1024 + 1024]
            pav = psf[:, 2048:3072]
            wt(ve, s_t, tstate[("avh", i)])
            wt(ve, s_v, tstate[("rec", i)])
            wt(ve, s_t, tstate.get(("avsfree", par), 0))
            vx.tensor_scalar(avs[:, 0:512], pav[:, 0:512], rec, None, ALU.mult)
            avs0_done = s_v.n
            wt(ve, s_t, tstate[("av", i)])
            vx.tensor_scalar(avs[:, 512:1024], pav[:, 512:1024], rec, None, ALU.mult)
            pav_free = s_v.n
            avs1_done = s_v.n
            wt(te, s_v, avs0_done)
            wt(te, s_v, psb_free)
            for tt in range(4):
                ins = te.transpose(psb[:, tt * 128:(tt + 1) * 128],
                                   avs[:, tt * 128:(tt + 1) * 128], ident[:])
            inc(ins, s_t)
            wt(te, s_v, avs1_done)
            for tt in range(4, 8):
                ins = te.transpose(psb[:, tt * 128:(tt + 1) * 128],
                                   avs[:, tt * 128:(tt + 1) * 128], ident[:])
            inc(ins, s_t)
            tstate[("avsfree", par)] = s_t.n
            avT_done = s_t.n
            xslot = FP[:, (F_XC0 if par == 0 else F_XC1):][:, 0:1024]
            gslot = BP[:, (A_X0 if par == 0 else A_SQ0):][:, 0:1024]
            ut = FP[:, (F_UT0 if par == 0 else F_UT1):][:, 0:1024]
            wt(ve, s_t, avT_done)
            wt(ve, s_ug[par])
            vx.tensor_mul(ut, psb[:, 0:1024], gslot)
            psb_free = s_v.n
            tstate[("updg", par)] = s_v.n
            oslot = FP[:, (F_O0 if par == 0 else F_O1):][:, 0:1024]
            wt(ve, s_ux[par])
            wt(ve, s_uo[par], tstate.get(("updo", par), 0))
            vx.tensor_add(oslot, xslot, ut)
            tstate[("updx", par)] = s_v.n
            wt(gp, s_v, tstate[("updx", par)])
            oap = out[:, i * 128:(i + 1) * 128].rearrange(
                "(e p) j -> p e j", e=8, p=128)
            inc(gp.dma_start(oap, oslot.rearrange("p (e j) -> p e j", e=8, j=128)),
                s_uo[par], 16)
            tstate[("updo", par)] = s_uo[par].n
            if i + 2 < NTI:
                pf_q(i + 2, par)
                pf_upd_tile(i + 2, par)

        def at_I(i):
            par = i % 2
            sums = FP[:, F_MISC + par * 8 + 2: F_MISC + par * 8 + 3]
            rec = FP[:, F_MISC + par * 8 + 3: F_MISC + par * 8 + 4]
            wt(ve, s_s, tstate[("exp", i)])
            vx.reciprocal(rec, sums)
            tstate[("rec", i)] = s_v.n

        pf_upd_tile(0, 0)
        pf_upd_tile(1, 1)
        for i in range(NTI):
            if i >= 1:
                at_AB(i - 1)
            at_CD(i)
            if i >= 1:
                at_E(i - 1)
            at_F(i)
            if i >= 1:
                at_GHJ(i - 1)
            at_I(i)
        at_AB(NTI - 1)
        at_E(NTI - 1)
        at_GHJ(NTI - 1)
        s_ready = s_v.n
        attn_t_done = s_t.n

        wt(sy, s_uo[0])
        wt(sy, s_uo[1])

    return nc


def pack_w(W, nM, nK):
    W = np.asarray(W)
    W4 = W.reshape(nM, 128, nK, 128)
    return np.ascontiguousarray(W4.transpose(0, 3, 2, 1)).reshape(nM, 128, nK * 128)


def make_weight_inputs(L0_attn_Wz, L0_attn_bz, L0_attn_Wh, L0_attn_bh,
                       L0_ffn_w_gate, L0_ffn_w_up, L0_ffn_w_down, L1_attn_W_qkv,
                       L1_ffn_W_r, L1_ffn_W_k, L1_ffn_W_v, L1_ffn_mu_r,
                       L1_ffn_mu_k):
    d = {}
    d["wz"] = pack_w(np.asarray(L0_attn_Wz, np.float32), NCH, NCH)
    d["wh"] = pack_w(np.asarray(L0_attn_Wh, np.float32), NCH, NCH)
    d["bzd"] = np.ascontiguousarray(np.asarray(L0_attn_bz, np.float32).reshape(NCH, 128).T)
    d["bhd"] = np.ascontiguousarray(np.asarray(L0_attn_bh, np.float32).reshape(NCH, 128).T)
    d["wg"] = pack_w(np.asarray(L0_ffn_w_gate, np.int32), NHID, NCH)
    d["wu"] = pack_w(np.asarray(L0_ffn_w_up, np.int32), NHID, NCH)
    d["wd"] = pack_w(np.asarray(L0_ffn_w_down, np.int32), NCH, NHID)
    d["wqkv"] = pack_w(np.asarray(L1_attn_W_qkv, np.float32), 3 * NCH, NCH)
    d["wr"] = pack_w(np.asarray(L1_ffn_W_r, np.float32), NCH, NCH)
    d["wk"] = pack_w(np.asarray(L1_ffn_W_k, np.float32), NCH, NCH)
    d["wv"] = pack_w(np.asarray(L1_ffn_W_v, np.float32), NCH, NCH)
    d["murd"] = np.ascontiguousarray(np.asarray(L1_ffn_mu_r, np.float32).reshape(NCH, 128).T)
    d["mukd"] = np.ascontiguousarray(np.asarray(L1_ffn_mu_k, np.float32).reshape(NCH, 128).T)
    d["onesd"] = np.ones((128, 128), dtype=ml_dtypes.bfloat16)
    d["identd"] = np.eye(128, dtype=ml_dtypes.bfloat16)
    return d


def kernel(L0_attn_x, L0_attn_Wz, L0_attn_bz, L0_attn_Wh, L0_attn_bh,
           L0_ffn_w_gate, L0_ffn_w_up, L0_ffn_w_down, L1_attn_W_qkv,
           L1_ffn_W_r, L1_ffn_W_k, L1_ffn_W_v, L1_ffn_mu_r, L1_ffn_mu_k):
    x = np.asarray(L0_attn_x, np.float32)
    B = x.shape[0]
    assert B == 8
    base = make_weight_inputs(L0_attn_Wz, L0_attn_bz, L0_attn_Wh, L0_attn_bh,
                              L0_ffn_w_gate, L0_ffn_w_up, L0_ffn_w_down,
                              L1_attn_W_qkv, L1_ffn_W_r, L1_ffn_W_k, L1_ffn_W_v,
                              L1_ffn_mu_r, L1_ffn_mu_k)
    in_maps = []
    for c in range(B):
        m = dict(base)
        m["xt"] = np.ascontiguousarray(x[c].T)
        in_maps.append(m)
    nc = build_program()
    res = run_bass_kernel_spmd(nc, in_maps, core_ids=list(range(8)))
    outs = [np.ascontiguousarray(np.asarray(r["out"]).T) for r in res.results]
    return np.stack(outs, axis=0).astype(np.float32)


# version 37
# speedup vs baseline: 1.0298x; 1.0298x over previous
import contextlib
import numpy as np
import ml_dtypes
import concourse.bass as bass
import concourse.mybir as mybir
from concourse.bass_utils import run_bass_kernel_spmd

F32 = mybir.dt.float32
BF16 = mybir.dt.bfloat16
I32 = mybir.dt.int32
AF = mybir.ActivationFunctionType
ALU = mybir.AluOpType
AX = mybir.AxisListType

D, T, H = 1024, 2048, 3072
NCH, NHID, NTI = 8, 24, 16
QSCALE = float(D) ** -0.5
FSCALE = 1.0 / (127.0 * 127.0)
EPS_LN = 1e-5
EPS_A = 1e-8

R0 = 0
R1 = 16384
R2 = 32768
A_W0 = 49152
A_W1 = 50176
A_V0 = 51200
A_V1 = 52224
A_WD0 = 53248
A_WD1 = 56320
A_S1 = 59392
A_S1b = 61440
A_S2 = 63488
A_S2b = 65536
A_T0 = 67584
A_T1 = 68608
A_X0 = 69632
A_SQ0 = 70656
BP_COLS = 71680

F_XS0, F_XS1 = 0, 4096
F_ST0, F_ST1 = 8192, 10240
F_TMP = 12288
F_SGM, F_A, F_HS, F_BNEG, F_SNEG = 0, 2048, 4096, 6144, 8192
F_XCm0, F_XCm1 = 10240, 12288
F_Om = 14336
F_XC0, F_XC1, F_UT0, F_UT1, F_O0, F_O1 = 0, 2048, 4096, 6144, 8192, 10240
F_MISC = 16384
F_COLS = 16384 + 64


class Sem:
    __slots__ = ("h", "n", "name")

    def __init__(self, h, name):
        self.h = h
        self.n = 0
        self.name = name


def build_program(n_layers=2):
    nc = bass.Bass("TRN2", target_bir_lowering=False)
    te, ve, se, sy, gp = nc.tensor, nc.vector, nc.scalar, nc.sync, nc.gpsimd
    sc = nc.scalar

    xt = nc.dram_tensor("xt", [D, T], F32, kind="ExternalInput").ap()
    wz = nc.dram_tensor("wz", [NCH, 128, D], F32, kind="ExternalInput").ap()
    wh = nc.dram_tensor("wh", [NCH, 128, D], F32, kind="ExternalInput").ap()
    bzd = nc.dram_tensor("bzd", [128, NCH], F32, kind="ExternalInput").ap()
    bhd = nc.dram_tensor("bhd", [128, NCH], F32, kind="ExternalInput").ap()
    wg = nc.dram_tensor("wg", [NHID, 128, D], I32, kind="ExternalInput").ap()
    wu = nc.dram_tensor("wu", [NHID, 128, D], I32, kind="ExternalInput").ap()
    wd = nc.dram_tensor("wd", [NCH, 128, H], I32, kind="ExternalInput").ap()
    wqkv = nc.dram_tensor("wqkv", [3 * NCH, 128, D], F32, kind="ExternalInput").ap()
    wr = nc.dram_tensor("wr", [NCH, 128, D], F32, kind="ExternalInput").ap()
    wk = nc.dram_tensor("wk", [NCH, 128, D], F32, kind="ExternalInput").ap()
    wv = nc.dram_tensor("wv", [NCH, 128, D], F32, kind="ExternalInput").ap()
    murd = nc.dram_tensor("murd", [128, NCH], F32, kind="ExternalInput").ap()
    mukd = nc.dram_tensor("mukd", [128, NCH], F32, kind="ExternalInput").ap()
    onesd = nc.dram_tensor("onesd", [128, 128], BF16, kind="ExternalInput").ap()
    identd = nc.dram_tensor("identd", [128, 128], BF16, kind="ExternalInput").ap()
    out = nc.dram_tensor("out", [D, T], F32, kind="ExternalOutput").ap()
    x1s = nc.dram_tensor("x1s", [NCH, 128, T], F32, kind="Internal").ap()
    gs = nc.dram_tensor("gs", [NCH, 128, T], BF16, kind="Internal").ap()
    xnbs = nc.dram_tensor("xnbs", [NCH, 128, T], BF16, kind="Internal").ap()
    qs = nc.dram_tensor("qs", [NCH, 128, T], BF16, kind="Internal").ap()

    es = contextlib.ExitStack()
    BP = es.enter_context(nc.sbuf_tensor("BP", [128, BP_COLS], BF16))
    FP = es.enter_context(nc.sbuf_tensor("FP", [128, F_COLS], F32))
    ones = es.enter_context(nc.sbuf_tensor("ones", [128, 128], BF16))
    ident = es.enter_context(nc.sbuf_tensor("ident", [128, 128], BF16))
    CB = es.enter_context(nc.sbuf_tensor("CB", [128, 44], F32))
    psf = es.enter_context(nc.psum_tensor("psf", [128, 3072], F32))
    psb = es.enter_context(nc.psum_tensor("psb", [128, 2048], BF16))

    def mksem(name):
        return Sem(es.enter_context(nc.semaphore(name)), name)

    s_t = mksem("s_t")
    s_v = mksem("s_v")
    s_s = mksem("s_s")
    s_c = mksem("s_c")
    s_ln = [mksem("s_ln0"), mksem("s_ln1")]
    s_w = [mksem("s_w0"), mksem("s_w1")]
    s_wd = [mksem("s_wd0"), mksem("s_wd1")]
    s_go = [mksem("s_go0"), mksem("s_go1")]
    s_gi = [mksem("s_gi0"), mksem("s_gi1")]
    s_xc = [mksem("s_xc0"), mksem("s_xc1")]
    s_x1 = [mksem("s_x10"), mksem("s_x11")]
    s_gp = mksem("s_gp")
    s_no = mksem("s_no")
    s_sp = [mksem(f"s_sp{i}") for i in range(4)]
    s_ni = [mksem("s_ni0"), mksem("s_ni1")]
    s_qo = [mksem("s_qo0"), mksem("s_qo1")]
    s_qi = [mksem("s_qi0"), mksem("s_qi1")]
    s_ux = [mksem("s_ux0"), mksem("s_ux1")]
    s_ug = [mksem("s_ug0"), mksem("s_ug1")]
    s_uo = [mksem("s_uo0"), mksem("s_uo1")]
    s_ou = [mksem("s_ou0"), mksem("s_ou1")]
    s_qw = mksem("s_qw")

    def inc(instr, sem, k=1):
        instr.then_inc(sem.h, k)
        sem.n += k

    def wt(eng, sem, val=None):
        v = sem.n if val is None else val
        assert v <= sem.n, f"wait {sem.name} {v} > {sem.n}"
        if v > 0:
            eng.wait_ge(sem.h, v)

    class EngProxy:
        def __init__(self, eng, sem):
            self._eng = eng
            self._sem = sem

        def __getattr__(self, name):
            fn = getattr(self._eng, name)
            if name in ("dma_start", "dma_start_transpose", "wait_ge"):
                return fn

            def wrapped(*a, **k):
                wt(self._eng, self._sem)
                ins = fn(*a, **k)
                inc(ins, self._sem)
                return ins
            return wrapped

    vx = EngProxy(ve, s_v)
    sx = EngProxy(se, s_s)
    gx = EngProxy(gp, s_gp)

    with es:
        inc(sy.dma_start(ones[:], onesd), s_c, 16)
        inc(sy.dma_start(ident[:], identd), s_c, 16)
        inc(sy.dma_start(CB[:, 32:40], bzd), s_c, 16)
        inc(sy.dma_start(CB[:, 8:16], bhd), s_c, 16)
        inc(sy.dma_start(CB[:, 16:24], murd), s_c, 16)
        inc(sy.dma_start(CB[:, 24:32], mukd), s_c, 16)
        wt(ve, s_c)
        vx.memset(CB[:, 40:41], EPS_LN)
        vx.tensor_scalar(CB[:, 0:8], CB[:, 32:40], -1.0, None, ALU.mult)
        wt(se, s_v)
        wt(te, s_c)

        def ln_build(layer, mix_hook=None):
            st = {"aff": {}, "statfree": {}, "castuse": {}, "stslot": {},
                  "spill": {}}
            src_x1 = (layer == 1)
            entry_v, entry_s = s_v.n, s_s.n
            wt(te, s_v)
            wt(te, s_s)
            psx = psf[:, 2048:2560]
            psq = psf[:, 2560:3072]

            def LD(sp):
                par = sp % 2
                xsb = F_XS0 if par == 0 else F_XS1
                if sp >= 2:
                    wt(sy, s_v, st["aff"][sp - 2])
                else:
                    wt(sy, s_v, entry_v)
                    wt(sy, s_s, entry_s)
                if src_x1:
                    wt(sy, s_x1[0])
                    wt(sy, s_x1[1])
                for c in range(NCH):
                    srcap = (x1s[c, :, sp * 512:(sp + 1) * 512] if src_x1 else
                             xt[c * 128:(c + 1) * 128, sp * 512:(sp + 1) * 512])
                    ins = sy.dma_start(FP[:, xsb + c * 512: xsb + (c + 1) * 512], srcap)
                    inc(ins, s_ln[par], 16)

            def CSMM(sp):
                par = sp % 2
                xsb = F_XS0 if par == 0 else F_XS1
                xb0f = BP[:, A_S1: A_S1 + 4096]
                sqf = BP[:, A_S2: A_S2 + 4096]
                xsf = FP[:, xsb: xsb + 4096]
                wt(ve, s_ln[par])
                wt(ve, s_t, st["castuse"].get(sp - 1, 0))
                vx.tensor_copy(xb0f, xsf)
                wt(se, s_ln[par])
                wt(se, s_t, st["castuse"].get(sp - 1, 0))
                sx.activation(sqf, xsf, AF.Square)
                wt(te, s_s)
                wt(te, s_v)
                if sp >= 1:
                    wt(te, s_v, st["statfree"][sp - 1])
                for c in range(NCH):
                    te.matmul(psx, ones[:], xb0f[:, c * 512:(c + 1) * 512],
                              start=(c == 0), stop=(c == NCH - 1))
                    ins = te.matmul(psq, ones[:], sqf[:, c * 512:(c + 1) * 512],
                                    start=(c == 0), stop=(c == NCH - 1))
                inc(ins, s_t)
                st["castuse"][sp] = s_t.n

            def EP(sp):
                par = sp % 2
                stb = F_ST0 if par == 0 else F_ST1
                mub = FP[:, stb: stb + 512]
                m2 = FP[:, stb + 512: stb + 1024]
                varb = FP[:, stb + 1024: stb + 1536]
                rstd = FP[:, stb + 1536: stb + 2048]
                wt(ve, s_t)
                if sp >= 2:
                    wt(ve, s_s, st["stslot"][sp - 2])
                vx.tensor_scalar(mub, psx, 1.0 / D, None, ALU.mult)
                if src_x1:
                    vx.tensor_mul(m2, mub, mub)
                else:
                    wt(se, s_v)
                    if sp >= 2:
                        wt(se, s_v, st["aff"][sp - 2])
                    sx.activation(m2, mub, AF.Square)
                    wt(ve, s_s)
                vx.scalar_tensor_tensor(varb, psq, 1.0 / D, m2,
                                        ALU.mult, ALU.subtract)
                st["statfree"][sp] = s_v.n
                wt(se, s_v)
                sx.activation(varb, varb, AF.Ln, bias=CB[:, 40:41])
                sx.activation(rstd, varb, AF.Exp, scale=-0.5)
                st["stslot"][sp] = s_s.n

            def AFN(sp):
                par = sp % 2
                xsb = F_XS0 if par == 0 else F_XS1
                stb = F_ST0 if par == 0 else F_ST1
                mub = FP[:, stb: stb + 512]
                rstd = FP[:, stb + 1536: stb + 2048]
                wt(ve, s_s, st["stslot"][sp])
                xs_full = FP[:, xsb: xsb + 4096].rearrange(
                    "p (c j) -> p c j", c=8, j=512)
                tmpf = FP[:, 12288: 12288 + 4096].rearrange(
                    "p (c j) -> p c j", c=8, j=512)
                mub_b = bass.broadcast_tensor_aps(
                    xs_full, mub.rearrange("p (c j) -> p c j", c=1, j=512))[1]
                rstd_b = bass.broadcast_tensor_aps(
                    xs_full, rstd.rearrange("p (c j) -> p c j", c=1, j=512))[1]
                dstf = BP[:, R0: R0 + 8 * T].rearrange(
                    "p (c j) -> p c j", c=8, j=T)[:, :, sp * 512:(sp + 1) * 512]
                vx.tensor_sub(tmpf, xs_full, mub_b)
                vx.tensor_mul(dstf, tmpf, rstd_b)
                st["aff"][sp] = s_v.n
                if src_x1:
                    wt(sy, s_v)
                    for c in range(NCH):
                        inc(sy.dma_start(
                            xnbs[c][:, sp * 512:(sp + 1) * 512],
                            BP[:, R0 + c * T + sp * 512:
                               R0 + c * T + (sp + 1) * 512]), s_sp[sp], 16)

            stages = [lambda: LD(0), lambda: LD(1), lambda: CSMM(0), lambda: EP(0),
                      lambda: CSMM(1), lambda: AFN(0), lambda: LD(2), lambda: EP(1),
                      lambda: AFN(1), lambda: LD(3), lambda: CSMM(2), lambda: EP(2),
                      lambda: AFN(2), lambda: CSMM(3), lambda: EP(3), lambda: AFN(3)]
            return {"stages": stages, "st": st}

        ln0 = ln_build(0)
        for f in ln0["stages"][:10]:
            f()
        ln0_suffix = ln0["stages"][10:]

        wfree = {0: 0, 1: 0}
        dfree = {0: 0, 1: 0}

        def load_tiles(entries, par, sem, gate):
            wt(gp, s_t, gate)
            for dram3, idx, col, w in entries:
                inc(gp.dma_start(BP[:, col: col + w], dram3[idx]), sem, 16)

        wt(te, s_v, ln0["st"]["aff"][0])
        q_free = [0, 0, 0, 0]
        ffn_q_gate = {1: ln0["st"]["aff"][1]}
        hsil_free = [0, 0]
        dn_free = [0, 0, 0]
        go_mark = [0, 0]
        half_gdone = 0
        def pf_gu(m, par):
            wslot = A_W0 if par == 0 else A_W1
            vslot = A_V0 if par == 0 else A_V1
            load_tiles([(wg, m, wslot, D), (wu, m, vslot, D)], par, s_w[par],
                       wfree[par])

        def pf_dn(o, par):
            dslot = A_WD0 if par == 0 else A_WD1
            load_tiles([(wd, o, dslot, H)], par, s_wd[par], dfree[par])


        def gcol(e):
            if e < 4:
                return R2 + 8192 + e * 2048
            if e < 6:
                return A_S1 if e == 4 else A_S1b
            return A_S2 if e == 6 else A_S2b

        gsl0_mark = {}
        gsl1_mark = {}
        ln0_sched = {3: 0, 5: 1, 7: 2, 9: 3, 11: 4, 13: 5}
        for half in range(2):
            t0 = half * 1024
            if half == 1:
                wt(te, s_s, half_gdone)
                wt(te, s_v, ln0["st"]["aff"][3])
            pf_gu(0, 0)
            pf_gu(1, 1)
            pf_dn(0, 0)
            pf_dn(1, 1)
            for m in range(NHID):
                if half == 0 and m in ln0_sched:
                    ln0_suffix[ln0_sched[m]]()
                par = m % 2
                wslot = A_W0 if par == 0 else A_W1
                vslot = A_V0 if par == 0 else A_V1
                wt(te, s_w[par])
                for sp in range(2):
                    qg, qu = sp * 2, sp * 2 + 1
                    pg = psf[:, qg * 512:(qg + 1) * 512]
                    pu = psf[:, qu * 512:(qu + 1) * 512]
                    if half == 0 and m == 0 and sp == 1:
                        wt(te, s_v, ffn_q_gate[1])
                    wt(te, s_v, q_free[qg])
                    for k in range(NCH):
                        lhsT = BP[:, wslot + k * 128: wslot + (k + 1) * 128]
                        rhs = BP[:, R0 + k * T + t0 + sp * 512:
                                 R0 + k * T + t0 + sp * 512 + 512]
                        te.matmul(pg, lhsT, rhs, start=(k == 0), stop=(k == NCH - 1))
                    wt(te, s_v, q_free[qu])
                    for k in range(NCH):
                        lhsT = BP[:, vslot + k * 128: vslot + (k + 1) * 128]
                        rhs = BP[:, R0 + k * T + t0 + sp * 512:
                                 R0 + k * T + t0 + sp * 512 + 512]
                        ins = te.matmul(pu, lhsT, rhs, start=(k == 0), stop=(k == NCH - 1))
                    inc(ins, s_t)
                    hsil = (BP[:, A_T0 + sp * 512: A_T0 + sp * 512 + 512] if par == 0
                            else BP[:, A_T1 + sp * 512: A_T1 + sp * 512 + 512])
                    wt(se, s_t)
                    wt(se, s_v, hsil_free[par])
                    sx.activation(hsil, pg, AF.Silu, scale=1.0 / 127.0)
                    wt(ve, s_s)
                    hcol = (R1 + m * 1024) if m < 16 else (R2 + (m - 16) * 1024)
                    hdst = BP[:, hcol + sp * 512: hcol + sp * 512 + 512]
                    vx.tensor_mul(hdst, hsil, pu)
                    q_free[qg] = s_v.n
                    q_free[qu] = s_v.n
                hsil_free[par] = s_v.n
                wfree[par] = s_t.n
                if m + 2 < NHID:
                    pf_gu(m + 2, par)
            hdone = s_v.n
            wt(te, s_v, ln0["st"]["statfree"][3])
            if half == 0:
                for o in range(NCH):
                    par = o % 2
                    dslot = A_WD0 if par == 0 else A_WD1
                    wt(te, s_wd[par])
                    wt(te, s_v, hdone)
                    u = o % 3
                    pd = psf[:, u * 1024:(u + 1) * 1024]
                    wt(te, s_s, dn_free[u])
                    for sp in range(2):
                        for m in range(NHID):
                            lhsT = BP[:, dslot + m * 128: dslot + (m + 1) * 128]
                            hcol = (R1 + m * 1024) if m < 16 else (R2 + (m - 16) * 1024)
                            rhs = BP[:, hcol + sp * 512: hcol + sp * 512 + 512]
                            ins = te.matmul(pd[:, sp * 512: sp * 512 + 512], lhsT, rhs,
                                            start=(m == 0), stop=(m == NHID - 1))
                    inc(ins, s_t)
                    dfree[par] = s_t.n
                    if o + 2 < NCH:
                        pf_dn(o + 2, par)
                    gdst = BP[:, gcol(o): gcol(o) + 1024]
                    wt(se, s_t)
                    sx.activation(gdst, pd, AF.Silu, scale=FSCALE)
                    dn_free[u] = s_s.n
                    gsl0_mark[o] = s_s.n
                half_gdone = s_s.n
                continue
            MF_SGM = [0, 2048]
            MF_BNEG = [4096, 6144]
            MF_SNEG = [8192, 10240]
            MF_XCM = [12288, 14336]
            entry_v = s_v.n
            scanmark = [entry_v, entry_v]
            xcmark = [0, 0]
            addmark = [0, 0]
            dn1_free = s_s.n
            mu_free = [s_s.n, s_s.n]

            def pf_mxw(e, par):
                wslot = A_W0 if par == 0 else A_W1
                vslot = A_V0 if par == 0 else A_V1
                load_tiles([(wz, e, wslot, D), (wh, e, vslot, D)], par, s_w[par],
                           wfree[par])

            def pf_x(e, par):
                xcs_ = FP[:, MF_XCM[par]: MF_XCM[par] + T]
                wt(sy, s_v, xcmark[par])
                wt(sy, s_v, entry_v if e < 2 else 0)
                inc(sy.dma_start(xcs_, xt[e * 128:(e + 1) * 128, :]), s_xc[par], 16)

            pf_mxw(0, 0)
            pf_mxw(1, 1)
            pf_x(0, 0)
            pf_x(1, 1)
            for o in range(NCH):
                par = o % 2
                dslot = A_WD0 if par == 0 else A_WD1
                wt(te, s_wd[par])
                wt(te, s_v, hdone)
                pd = psf[:, 0:1024]
                wt(te, s_s, dn1_free)
                for sp in range(2):
                    for m in range(NHID):
                        lhsT = BP[:, dslot + m * 128: dslot + (m + 1) * 128]
                        hcol = (R1 + m * 1024) if m < 16 else (R2 + (m - 16) * 1024)
                        rhs = BP[:, hcol + sp * 512: hcol + sp * 512 + 512]
                        ins = te.matmul(pd[:, sp * 512: sp * 512 + 512], lhsT, rhs,
                                        start=(m == 0), stop=(m == NHID - 1))
                inc(ins, s_t)
                dfree[par] = s_t.n
                if o + 2 < NCH:
                    pf_dn(o + 2, par)
                gdst = BP[:, gcol(o) + 1024: gcol(o) + 2048]
                wt(se, s_t)
                sx.activation(gdst, pd, AF.Silu, scale=FSCALE)
                dn1_free = s_s.n
                gsl1_mark[o] = s_s.n
                e = o
                wslot = A_W0 if par == 0 else A_W1
                vslot = A_V0 if par == 0 else A_V1
                sgm = FP[:, MF_SGM[par]: MF_SGM[par] + T]
                bneg = FP[:, MF_BNEG[par]: MF_BNEG[par] + T]
                sneg = FP[:, MF_SNEG[par]: MF_SNEG[par] + T]
                xcs = FP[:, MF_XCM[par]: MF_XCM[par] + T]
                wt(te, s_w[par])
                for j in range(4):
                    mslot = wslot if (j % 2 == 0) else vslot
                    hh = j // 2
                    u2 = (e * 4 + j) % 2
                    pu_ap = psf[:, 1024 + u2 * 1024: 2048 + u2 * 1024]
                    wt(te, s_s, mu_free[u2])
                    for k in range(NCH):
                        lhsT = BP[:, mslot + k * 128: mslot + (k + 1) * 128]
                        for sps in range(2):
                            rhs = BP[:, R0 + k * T + hh * 1024 + sps * 512:
                                     R0 + k * T + hh * 1024 + sps * 512 + 512]
                            ins = te.matmul(pu_ap[:, sps * 512: sps * 512 + 512],
                                            lhsT, rhs,
                                            start=(k == 0), stop=(k == NCH - 1))
                    inc(ins, s_t)
                    wt(se, s_t)
                    wt(se, s_v, scanmark[par])
                    if j % 2 == 0:
                        dst = FP[:, MF_SGM[par] + hh * 1024:
                                 MF_SGM[par] + (hh + 1) * 1024]
                        sx.activation(dst, pu_ap, AF.Sigmoid, scale=-1.0,
                                      bias=CB[:, e:e + 1])
                    else:
                        dst = FP[:, MF_BNEG[par] + hh * 1024:
                                 MF_BNEG[par] + (hh + 1) * 1024]
                        sx.activation(dst, pu_ap, AF.Identity, bias=CB[:, 8 + e:9 + e])
                    mu_free[u2] = s_s.n
                evac_done = s_s.n
                wt(ve, s_s, evac_done)
                vx.scalar_tensor_tensor(bneg, sgm, 1.0, bneg, ALU.subtract, ALU.mult)
                vx.tensor_scalar(sgm, sgm, EPS_A, None, ALU.add)
                wt(ve, s_x1[par])
                vx.tensor_tensor_scan(sneg, sgm, bneg, 0.0, ALU.mult, ALU.add)
                scanmark[par] = s_v.n
                gin = BP[:, gcol(e): gcol(e) + T]
                wt(ve, s_s, gsl1_mark[e])
                vx.scalar_tensor_tensor(sneg, sneg, -1.0, gin, ALU.mult, ALU.mult)
                wt(ve, s_xc[par])
                vx.tensor_add(sneg, sneg, xcs)
                xcmark[par] = s_v.n
                addmark[par] = s_v.n
                wt(sy, s_v, addmark[par])
                inc(sy.dma_start(x1s[e], sneg), s_x1[par], 16)
                wfree[par] = s_t.n
                if e + 2 < NCH:
                    pf_mxw(e + 2, par)
                    pf_x(e + 2, par)

        if n_layers == 1:
            wt(sy, s_x1[0])
            wt(sy, s_x1[1])
            for c in range(NCH):
                inc(sy.dma_start(out[c * 128:(c + 1) * 128, :], x1s[c]),
                    s_ou[c % 2], 16)
            wt(sy, s_ou[0])
            wt(sy, s_ou[1])
            return nc

        def mix_span(sp, c):
            a = sp * 512
            xnc = BP[:, R0 + c * T: R0 + (c + 1) * T]
            xr_c = BP[:, R1 + c * T: R1 + (c + 1) * T]
            xk_c = BP[:, R2 + c * T: R2 + (c + 1) * T]
            dtmp = BP[:, A_X0: A_X0 + 512]
            if sp == 0:
                vx.tensor_sub(dtmp[:, 1:512], xnc[:, 1:512], xnc[:, 0:511])
                vx.scalar_tensor_tensor(xr_c[:, 1:512], dtmp[:, 1:512],
                                        CB[:, 16 + c:17 + c], xnc[:, 0:511],
                                        ALU.mult, ALU.add)
                vx.tensor_scalar(xr_c[:, 0:1], xnc[:, 0:1], CB[:, 16 + c:17 + c],
                                 None, ALU.mult)
                vx.scalar_tensor_tensor(xk_c[:, 1:512], dtmp[:, 1:512],
                                        CB[:, 24 + c:25 + c], xnc[:, 0:511],
                                        ALU.mult, ALU.add)
                vx.tensor_scalar(xk_c[:, 0:1], xnc[:, 0:1], CB[:, 24 + c:25 + c],
                                 None, ALU.mult)
            elif sp == 2:
                bc = BP[:, A_X0 + 512 + c: A_X0 + 513 + c]
                prev = xnc[:, a: a + 511]
                vx.tensor_sub(dtmp[:, 0:1], xnc[:, a: a + 1], bc)
                vx.tensor_sub(dtmp[:, 1:512], xnc[:, a + 1: a + 512], prev)
                vx.scalar_tensor_tensor(xr_c[:, a: a + 1], dtmp[:, 0:1],
                                        CB[:, 16 + c:17 + c], bc,
                                        ALU.mult, ALU.add)
                vx.scalar_tensor_tensor(xr_c[:, a + 1: a + 512], dtmp[:, 1:512],
                                        CB[:, 16 + c:17 + c], prev,
                                        ALU.mult, ALU.add)
                vx.scalar_tensor_tensor(xk_c[:, a: a + 1], dtmp[:, 0:1],
                                        CB[:, 24 + c:25 + c], bc,
                                        ALU.mult, ALU.add)
                vx.scalar_tensor_tensor(xk_c[:, a + 1: a + 512], dtmp[:, 1:512],
                                        CB[:, 24 + c:25 + c], prev,
                                        ALU.mult, ALU.add)
            else:
                prev = xnc[:, a - 1: a + 511]
                vx.tensor_sub(dtmp, xnc[:, a: a + 512], prev)
                vx.scalar_tensor_tensor(xr_c[:, a: a + 512], dtmp,
                                        CB[:, 16 + c:17 + c], prev,
                                        ALU.mult, ALU.add)
                vx.scalar_tensor_tensor(xk_c[:, a: a + 512], dtmp,
                                        CB[:, 24 + c:25 + c], prev,
                                        ALU.mult, ALU.add)

        qw_cols = [A_WD0, A_WD0 + 1024, A_WD0 + 2048, A_WD1, A_WD1 + 1024,
                   A_WD1 + 2048, A_V0, A_V1]
        l0_t = s_t.n
        l0_v = s_v.n
        l0_go = [s_go[0].n, s_go[1].n]
        wt(gp, s_t, l0_t)
        for e in range(NCH):
            inc(gp.dma_start(BP[:, qw_cols[e]: qw_cols[e] + 1024], wqkv[e]),
                s_qw, 16)

        kseq = {0: [(0, 0), (0, 2), (0, 4), (0, 6), (1, 0), (1, 2), (1, 4),
                    (1, 6)],
                1: [(0, 1), (0, 3), (0, 5), (0, 7), (1, 1), (1, 3), (1, 5),
                    (1, 7)]}
        kload_i = {0: 0, 1: 0}

        def pf_wk2(par, gate):
            i = kload_i[par]
            if i < len(kseq[par]):
                _, o = kseq[par][i]
                load_tiles([(wk, o, A_W0 if par == 0 else A_W1, D)], par,
                           s_w[par], gate)
                kload_i[par] = i + 1

        pf_wk2(0, l0_t)
        pf_wk2(1, l0_t)

        ln1 = ln_build(1)
        st1 = ln1["st"]
        q_bank_free = [0, 0, 0, 0]
        qstg_free = [0, 0]
        mixm = {}

        def q_mm(sp, e):
            b = e % 4 if sp < 2 else e % 2
            pq = psf[:, b * 512: b * 512 + 512]
            wt(te, s_qw)
            wt(te, s_v, st1["aff"][sp])
            wt(te, s_s, q_bank_free[b])
            for c in range(NCH):
                lhsT = BP[:, qw_cols[e] + c * 128: qw_cols[e] + (c + 1) * 128]
                rhs = BP[:, R0 + c * T + sp * 512: R0 + c * T + sp * 512 + 512]
                ins = te.matmul(pq, lhsT, rhs, start=(c == 0), stop=(c == NCH - 1))
            inc(ins, s_t)

        def q_dr(sp, e):
            b = e % 4 if sp < 2 else e % 2
            pq = psf[:, b * 512: b * 512 + 512]
            par = e % 2
            stg = BP[:, (A_T0 if par == 0 else A_T1):][:, 0:512]
            wt(se, s_t)
            wt(se, s_v, l0_v)
            wt(se, s_go[par], l0_go[par])
            wt(se, s_qo[par], qstg_free[par])
            sx.activation(stg, pq, AF.Identity)
            q_bank_free[b] = s_s.n
            wt(sc, s_s)
            inc(sc.dma_start(qs[e, :, sp * 512:(sp + 1) * 512], stg),
                s_qo[par], 16)
            qstg_free[par] = s_qo[par].n

        def q_mma(sp):
            for e in range(4):
                q_mm(sp, e)

        def q_tail(sp):
            for e in range(4):
                q_dr(sp, e)
            for e in range(4, NCH):
                q_mm(sp, e)
            for e in range(4, NCH):
                q_dr(sp, e)

        def q_block(sp):
            for e in range(NCH):
                q_mm(sp, e)
                q_dr(sp, e)

        def mixes(sp):
            for c in range(NCH):
                mix_span(sp, c)
            mixm[sp] = s_v.n

        def bcol_save():
            src = BP[:, R0: R0 + 8 * T].rearrange(
                "p (c j) -> p c j", c=8, j=T)[:, :, 1023:1024]
            dst = BP[:, A_X0 + 512: A_X0 + 520].rearrange(
                "p (c j) -> p c j", c=8, j=1)
            vx.tensor_copy(dst, src)

        k_bank_free = {2: 0, 3: 0}
        kmarks = {}
        kact_done = 0

        def k_unit(hh, o):
            nonlocal kact_done
            par = o % 2
            wslot = A_W0 if par == 0 else A_W1
            rt = BP[:, (A_T0 if par == 0 else A_T1):][:, 512:1024]
            kdst = BP[:, R0 + o * T + hh * 1024: R0 + o * T + (hh + 1) * 1024]
            wt(te, s_w[par])
            wt(te, s_v, mixm[1 if hh == 0 else 3])
            if hh == 0 and o == 0:
                wt(te, s_s, q_bank_free[2])
                wt(te, s_s, q_bank_free[3])
            kmm = 0
            for half in range(2):
                bank = 2 + half
                pu = psf[:, bank * 512: bank * 512 + 512]
                wt(te, s_s, k_bank_free[bank])
                for k in range(NCH):
                    lhsT = BP[:, wslot + k * 128: wslot + (k + 1) * 128]
                    rhs = BP[:, R2 + k * T + hh * 1024 + half * 512:
                             R2 + k * T + hh * 1024 + half * 512 + 512]
                    ins = te.matmul(pu, lhsT, rhs, start=(k == 0),
                                    stop=(k == NCH - 1))
                inc(ins, s_t)
                kmm = s_t.n
                wt(se, s_t, kmm)
                sx.activation(rt, pu, AF.Relu)
                k_bank_free[bank] = s_s.n
                wt(se, s_v, mixm[1 if hh == 0 else 3])
                wt(se, s_sp[0 if hh == 0 else 2])
                wt(se, s_sp[1 if hh == 0 else 3])
                sx.activation(kdst[:, half * 512:(half + 1) * 512], rt,
                              AF.Square)
            pf_wk2(par, kmm)
            kmarks[(hh, o)] = kmm
            kact_done = s_s.n

        def post11():
            q_tail(1)
            for o in range(4):
                k_unit(0, o)

        def post15():
            mixes(3)
            q_block(3)
            for o in range(NCH):
                k_unit(1, o)

        post = {5: lambda: (mixes(0), q_mma(0)), 7: lambda: q_tail(0),
                8: lambda: (mixes(1), bcol_save(), q_mma(1)), 11: post11,
                12: lambda: (mixes(2), q_block(2)),
                14: lambda: [k_unit(0, o) for o in range(4, NCH)],
                15: post15}
        for idx, f in enumerate(ln1["stages"]):
            f()
            if idx in post:
                post[idx]()
        mix_done = s_v.n
        wfree = {0: kmarks[(1, 6)], 1: kmarks[(1, 7)]}

        wt(sc, s_t)
        for _i in range(4):
            wt(sc, s_sp[_i])
        for c in range(NCH):
            inc(sc.dma_start(BP[:, R2 + c * T: R2 + (c + 1) * T], xnbs[c]),
                s_ni[0], 16)

        wt(te, s_v)
        wt(te, s_s)
        wt(ve, s_qo[0])
        wt(ve, s_qo[1])
        kvfree = [0, 0]
        kvdone = [0, 0]
        rp_free = 0
        rt2_free = [mix_done, mix_done]
        ts_free = 0
        sg_free = [0, 0]
        go2_mark = [s_go[0].n, s_go[1].n]

        def pf_vr(o, par):
            load_tiles([(wv, o, A_W0 if par == 0 else A_W1, D),
                        (wr, o, A_V0 if par == 0 else A_V1, D)], par, s_w[par],
                       wfree[par])

        pf_vr(0, 0)
        pf_vr(1, 1)
        for o in range(NCH):
            par = o % 2
            wslot = A_W0 if par == 0 else A_W1
            vslot = A_V0 if par == 0 else A_V1
            wt(te, s_w[par])
            wt(te, s_s, kact_done)
            for piece in range(2):
                pv = psf[:, piece * 1024:(piece + 1) * 1024]
                wt(te, s_v, kvfree[piece])
                for k in range(NCH):
                    lhsT = BP[:, wslot + k * 128: wslot + (k + 1) * 128]
                    for sps in range(2):
                        rhs = BP[:, R0 + k * T + piece * 1024 + sps * 512:
                                 R0 + k * T + piece * 1024 + sps * 512 + 512]
                        ins = te.matmul(pv[:, sps * 512: sps * 512 + 512], lhsT, rhs,
                                        start=(k == 0), stop=(k == NCH - 1))
                inc(ins, s_t)
                kvdone[piece] = s_t.n
            for hh in range(2):
                pr = psf[:, 2048:3072]
                wt(te, s_s, rp_free)
                for k in range(NCH):
                    lhsT = BP[:, vslot + k * 128: vslot + (k + 1) * 128]
                    for sps in range(2):
                        rhs = BP[:, R1 + k * T + hh * 1024 + sps * 512:
                                 R1 + k * T + hh * 1024 + sps * 512 + 512]
                        ins = te.matmul(pr[:, sps * 512: sps * 512 + 512], lhsT, rhs,
                                        start=(k == 0), stop=(k == NCH - 1))
                inc(ins, s_t)
                rtmp = BP[:, (A_S2 if par == 0 else A_S2b):][:, hh * 1024:(hh + 1) * 1024]
                wt(se, s_t)
                wt(se, s_v, rt2_free[par])
                sx.activation(rtmp, pr, AF.Sigmoid)
                rp_free = s_s.n
                tslot = FP[:, F_TMP: F_TMP + 1024]
                wt(ve, s_s)
                wt(ve, s_t, kvdone[hh])
                wt(ve, s_s, ts_free)
                vx.tensor_mul(tslot, rtmp, psf[:, hh * 1024:(hh + 1) * 1024])
                rt2_free[par] = s_v.n
                kvfree[hh] = s_v.n
                sg = BP[:, (A_X0 if hh == 0 else A_SQ0):][:, 0:1024]
                wt(se, s_v)
                wt(se, s_v, sg_free[hh])
                sx.activation(sg, tslot, AF.Sigmoid)
                ts_free = s_s.n
                gtmp = BP[:, (A_T0 if par == 0 else A_T1):][:, 0:1024]
                wt(ve, s_s)
                wt(ve, s_go[par], go2_mark[par])
                vx.tensor_mul(gtmp, tslot, sg)
                sg_free[hh] = s_v.n
                wt(sc, s_v)
                inc(sc.dma_start(gs[o, :, hh * 1024:(hh + 1) * 1024], gtmp),
                    s_go[par], 16)
                go2_mark[par] = s_go[par].n
            wfree[par] = s_t.n
            if o + 2 < NCH:
                pf_vr(o + 2, par)
        g1_done = [s_go[0].n, s_go[1].n]
        rwkv_t_done = s_t.n

        wt(te, s_v)
        wt(te, s_s)
        u_free3 = [0, 0, 0]
        qtmp_free = [0, 0]
        vtp_free = {0: 0, 1: 0}

        def pf_qkv(e, par):
            load_tiles([(wqkv, e, A_W0 if par == 0 else A_W1, D)], par, s_w[par],
                       wfree[par])

        pf_qkv(NCH, 0)
        pf_qkv(NCH + 1, 1)
        for e in range(NCH, 3 * NCH):
            par = e % 2
            wslot = A_W0 if par == 0 else A_W1
            wt(te, s_w[par])
            wt(te, s_ni[0])
            vdefer = []
            for hh in range(2):
                u = (e * 2 + hh) % 3
                pu_ap = psf[:, u * 1024:(u + 1) * 1024]
                wt(te, s_v, u_free3[u])
                for k in range(NCH):
                    lhsT = BP[:, wslot + k * 128: wslot + (k + 1) * 128]
                    for sps in range(2):
                        rhs = BP[:, R2 + k * T + hh * 1024 + sps * 512:
                                 R2 + k * T + hh * 1024 + sps * 512 + 512]
                        ins = te.matmul(pu_ap[:, sps * 512: sps * 512 + 512], lhsT, rhs,
                                        start=(k == 0), stop=(k == NCH - 1))
                inc(ins, s_t)
                if e < NCH:
                    qtmp = BP[:, (A_S1 if par == 0 else A_S1b):][:, hh * 1024:(hh + 1) * 1024]
                    wt(ve, s_t)
                    wt(ve, s_qo[par], qtmp_free[par])
                    vx.tensor_copy(qtmp, pu_ap)
                    u_free3[u] = s_v.n
                    wt(sy, s_v)
                    inc(sy.dma_start(qs[e, :, hh * 1024:(hh + 1) * 1024], qtmp),
                        s_qo[par], 16)
                    if hh == 1:
                        qtmp_free[par] = s_qo[par].n
                elif e < 2 * NCH:
                    kdst = BP[:, R1 + (e - NCH) * T + hh * 1024:
                              R1 + (e - NCH) * T + (hh + 1) * 1024]
                    wt(ve, s_t)
                    vx.tensor_copy(kdst, pu_ap)
                    u_free3[u] = s_v.n
                else:
                    vtmp = BP[:, (A_S1 if par == 0 else A_S1b):][:, hh * 1024:(hh + 1) * 1024]
                    wt(ve, s_t)
                    wt(ve, s_qo[par], qtmp_free[par])
                    vx.tensor_copy(vtmp, pu_ap)
                    u_free3[u] = s_v.n
                    vdefer.append((hh, vtmp, s_v.n))
            for (hh, vtmp, vready) in vdefer:
                c = e - 2 * NCH
                wt(te, s_v, vready)
                wt(te, s_v, vtp_free[hh])
                for tt in range(8):
                    ins = te.transpose(
                        psb[:, hh * 1024 + tt * 128: hh * 1024 + (tt + 1) * 128],
                        vtmp[:, tt * 128:(tt + 1) * 128], ident[:])
                inc(ins, s_t)
                wt(ve, s_t)
                base = R0 + hh * 8192 + c * 128
                dst_ap = BP[:, base: base + 8192].rearrange(
                    "p (t j) -> p t j", t=8, j=1024)[:, :, 0:128]
                src_ap = psb[:, hh * 1024: hh * 1024 + 1024].rearrange(
                    "p (t j) -> p t j", t=8, j=128)
                vx.tensor_copy(dst_ap, src_ap)
                vtp_free[hh] = s_v.n
            wfree[par] = s_t.n
            if e + 2 < 3 * NCH:
                pf_qkv(e + 2, par)
        qkv_v_done = s_v.n

        wt(te, s_v)
        wt(te, s_s)
        wt(sy, s_go[0])
        wt(sy, s_go[1])
        exp_done = {0: 0, 1: 0}
        attb_free = {0: 0, 1: 0}
        mx_free = {0: 0, 1: 0}
        qt_free = {0: 0, 1: 0}
        psb_free = qkv_v_done
        pav_free = 0
        attT_free = {0: 0, 1: 0}
        sc_read = 0

        def pf_q(i, par):
            qt_ = BP[:, (A_T0 if par == 0 else A_T1):][:, 0:1024]
            wt(sy, s_qo[0])
            wt(sy, s_qo[1])
            wt(sy, s_t, qt_free[par])
            sap = qs[:, :, i * 128:(i + 1) * 128].rearrange("e p j -> p e j")
            inc(sy.dma_start(qt_.rearrange("p (e j) -> p e j", e=8, j=128), sap),
                s_qi[par], 16)

        pf_q(0, 0)
        pf_q(1, 1)
        tstate = {}

        tstate[("attT", -1)] = qkv_v_done

        def at_AB(i):
            nonlocal psb_free
            par = i % 2
            attb = BP[:, (A_S2 if par == 0 else A_S2b):][:, 0:T]
            wt(te, s_s, tstate[("exp", i)])
            wt(te, s_v, tstate[("attT", i - 1)])
            for tt in range(8, NTI):
                ins = te.transpose(psb[:, tt * 128:(tt + 1) * 128],
                                   attb[:, tt * 128:(tt + 1) * 128], ident[:])
            inc(ins, s_t)
            wt(te, s_v, psb_free)
            for tt in range(8):
                ins = te.transpose(psb[:, tt * 128:(tt + 1) * 128],
                                   attb[:, tt * 128:(tt + 1) * 128], ident[:])
            inc(ins, s_t)
            attb_free[par] = s_t.n
            attT = BP[:, (A_S1 if par == 0 else A_S1b):][:, 0:T]
            wt(ve, s_t)
            wt(ve, s_t, attT_free[par])
            vx.tensor_copy(attT, psb[:, 0:2048])
            tstate[("attT", i)] = s_v.n
            psb_free = s_v.n

        def at_CD(i):
            par = i % 2
            qt = BP[:, (A_T0 if par == 0 else A_T1):][:, 0:1024]
            psc = psf[:, 0:2048]
            wt(te, s_qi[par])
            wt(te, s_s, exp_done[0])
            wt(te, s_s, exp_done[1])
            wt(te, s_v, tstate.get(("rm", 0), 0))
            wt(te, s_v, tstate.get(("rm", 1), 0))
            for k in range(NCH):
                lhsT = qt[:, k * 128:(k + 1) * 128]
                for sps in range(4):
                    rhs = BP[:, R1 + k * T + sps * 512: R1 + k * T + sps * 512 + 512]
                    ins = te.matmul(psc[:, sps * 512: sps * 512 + 512], lhsT, rhs,
                                    start=(k == 0), stop=(k == NCH - 1))
            inc(ins, s_t)
            qt_free[par] = s_t.n
            mx = FP[:, F_MISC + par * 8: F_MISC + par * 8 + 1]
            bias = FP[:, F_MISC + par * 8 + 1: F_MISC + par * 8 + 2]
            wt(ve, s_t)
            wt(ve, s_s, mx_free[par])
            vx.reduce_max(mx, psc, axis=AX.X)
            vx.tensor_scalar(bias, mx, -QSCALE, None, ALU.mult)
            tstate[("rm", par)] = s_v.n

        def at_E(i):
            nonlocal pav_free
            par = i % 2
            attT = BP[:, (A_S1 if par == 0 else A_S1b):][:, 0:T]
            pav = psf[:, 2048:3072]
            wt(te, s_v, tstate[("attT", i)])
            wt(te, s_v, pav_free)
            for sps in range(2):
                for tt in range(NTI):
                    lhsT = attT[:, tt * 128:(tt + 1) * 128]
                    rhs = BP[:, R0 + tt * 1024 + sps * 512:
                             R0 + tt * 1024 + sps * 512 + 512]
                    ins = te.matmul(pav[:, sps * 512: sps * 512 + 512], lhsT, rhs,
                                    start=(tt == 0), stop=(tt == NTI - 1))
                inc(ins, s_t)
                if sps == 0:
                    tstate[("avh", i)] = s_t.n
            attT_free[par] = s_t.n
            tstate[("av", i)] = s_t.n

        def at_F(i):
            par = i % 2
            psc = psf[:, 0:2048]
            bias = FP[:, F_MISC + par * 8 + 1: F_MISC + par * 8 + 2]
            sums = FP[:, F_MISC + par * 8 + 2: F_MISC + par * 8 + 3]
            attb = BP[:, (A_S2 if par == 0 else A_S2b):][:, 0:T]
            wt(se, s_v, tstate[("rm", par)])
            wt(se, s_t, attb_free[par])
            sx.activation(attb, psc, AF.Exp, scale=QSCALE, bias=bias,
                          accum_out=sums)
            exp_done[par] = s_s.n
            mx_free[par] = s_s.n
            tstate[("exp", i)] = s_s.n

        def pf_upd_tile(i, par):
            xslot = FP[:, (F_XC0 if par == 0 else F_XC1):][:, 0:1024]
            wt(gp, s_v, tstate.get(("updx", par), 0))
            wt(gp, s_x1[0])
            wt(gp, s_x1[1])
            sap = x1s[:, :, i * 128:(i + 1) * 128].rearrange("e p j -> p e j")
            inc(gp.dma_start(xslot.rearrange("p (e j) -> p e j", e=8, j=128), sap),
                s_ux[par], 16)
            gslot = BP[:, (A_X0 if par == 0 else A_SQ0):][:, 0:1024]
            wt(gp, s_go[0], g1_done[0])
            wt(gp, s_go[1], g1_done[1])
            wt(gp, s_v, tstate.get(("updg", par), 0))
            gap_ = gs[:, :, i * 128:(i + 1) * 128].rearrange("e p j -> p e j")
            inc(gp.dma_start(gslot.rearrange("p (e j) -> p e j", e=8, j=128), gap_),
                s_ug[par], 16)

        def at_GHJ(i):
            nonlocal psb_free, pav_free
            par = i % 2
            qt = BP[:, (A_T0 if par == 0 else A_T1):][:, 0:1024]
            rec = FP[:, F_MISC + par * 8 + 3: F_MISC + par * 8 + 4]
            avs = BP[:, R2 + par * 1024: R2 + par * 1024 + 1024]
            pav = psf[:, 2048:3072]
            wt(ve, s_t, tstate[("avh", i)])
            wt(ve, s_v, tstate[("rec", i)])
            wt(ve, s_t, tstate.get(("avsfree", par), 0))
            vx.tensor_scalar(avs[:, 0:512], pav[:, 0:512], rec, None, ALU.mult)
            avs0_done = s_v.n
            wt(ve, s_t, tstate[("av", i)])
            vx.tensor_scalar(avs[:, 512:1024], pav[:, 512:1024], rec, None, ALU.mult)
            pav_free = s_v.n
            avs1_done = s_v.n
            wt(te, s_v, avs0_done)
            wt(te, s_v, psb_free)
            for tt in range(4):
                ins = te.transpose(psb[:, tt * 128:(tt + 1) * 128],
                                   avs[:, tt * 128:(tt + 1) * 128], ident[:])
            inc(ins, s_t)
            wt(te, s_v, avs1_done)
            for tt in range(4, 8):
                ins = te.transpose(psb[:, tt * 128:(tt + 1) * 128],
                                   avs[:, tt * 128:(tt + 1) * 128], ident[:])
            inc(ins, s_t)
            tstate[("avsfree", par)] = s_t.n
            avT_done = s_t.n
            xslot = FP[:, (F_XC0 if par == 0 else F_XC1):][:, 0:1024]
            gslot = BP[:, (A_X0 if par == 0 else A_SQ0):][:, 0:1024]
            ut = FP[:, (F_UT0 if par == 0 else F_UT1):][:, 0:1024]
            wt(ve, s_t, avT_done)
            wt(ve, s_ug[par])
            vx.tensor_mul(ut, psb[:, 0:1024], gslot)
            psb_free = s_v.n
            tstate[("updg", par)] = s_v.n
            oslot = FP[:, (F_O0 if par == 0 else F_O1):][:, 0:1024]
            wt(ve, s_ux[par])
            wt(ve, s_uo[par], tstate.get(("updo", par), 0))
            vx.tensor_add(oslot, xslot, ut)
            tstate[("updx", par)] = s_v.n
            wt(gp, s_v, tstate[("updx", par)])
            oap = out[:, i * 128:(i + 1) * 128].rearrange(
                "(e p) j -> p e j", e=8, p=128)
            inc(gp.dma_start(oap, oslot.rearrange("p (e j) -> p e j", e=8, j=128)),
                s_uo[par], 16)
            tstate[("updo", par)] = s_uo[par].n
            if i + 2 < NTI:
                pf_q(i + 2, par)
                pf_upd_tile(i + 2, par)

        def at_I(i):
            par = i % 2
            sums = FP[:, F_MISC + par * 8 + 2: F_MISC + par * 8 + 3]
            rec = FP[:, F_MISC + par * 8 + 3: F_MISC + par * 8 + 4]
            wt(ve, s_s, tstate[("exp", i)])
            vx.reciprocal(rec, sums)
            tstate[("rec", i)] = s_v.n

        pf_upd_tile(0, 0)
        pf_upd_tile(1, 1)
        for i in range(NTI):
            if i >= 1:
                at_AB(i - 1)
            at_CD(i)
            if i >= 1:
                at_E(i - 1)
            at_F(i)
            if i >= 1:
                at_GHJ(i - 1)
            at_I(i)
        at_AB(NTI - 1)
        at_E(NTI - 1)
        at_GHJ(NTI - 1)
        s_ready = s_v.n
        attn_t_done = s_t.n

        wt(sy, s_uo[0])
        wt(sy, s_uo[1])

    return nc


def pack_w(W, nM, nK):
    W = np.asarray(W)
    W4 = W.reshape(nM, 128, nK, 128)
    return np.ascontiguousarray(W4.transpose(0, 3, 2, 1)).reshape(nM, 128, nK * 128)


def make_weight_inputs(L0_attn_Wz, L0_attn_bz, L0_attn_Wh, L0_attn_bh,
                       L0_ffn_w_gate, L0_ffn_w_up, L0_ffn_w_down, L1_attn_W_qkv,
                       L1_ffn_W_r, L1_ffn_W_k, L1_ffn_W_v, L1_ffn_mu_r,
                       L1_ffn_mu_k):
    d = {}
    d["wz"] = pack_w(np.asarray(L0_attn_Wz, np.float32), NCH, NCH)
    d["wh"] = pack_w(np.asarray(L0_attn_Wh, np.float32), NCH, NCH)
    d["bzd"] = np.ascontiguousarray(np.asarray(L0_attn_bz, np.float32).reshape(NCH, 128).T)
    d["bhd"] = np.ascontiguousarray(np.asarray(L0_attn_bh, np.float32).reshape(NCH, 128).T)
    d["wg"] = pack_w(np.asarray(L0_ffn_w_gate, np.int32), NHID, NCH)
    d["wu"] = pack_w(np.asarray(L0_ffn_w_up, np.int32), NHID, NCH)
    d["wd"] = pack_w(np.asarray(L0_ffn_w_down, np.int32), NCH, NHID)
    d["wqkv"] = pack_w(np.asarray(L1_attn_W_qkv, np.float32), 3 * NCH, NCH)
    d["wr"] = pack_w(np.asarray(L1_ffn_W_r, np.float32), NCH, NCH)
    d["wk"] = pack_w(np.asarray(L1_ffn_W_k, np.float32), NCH, NCH)
    d["wv"] = pack_w(np.asarray(L1_ffn_W_v, np.float32), NCH, NCH)
    d["murd"] = np.ascontiguousarray(np.asarray(L1_ffn_mu_r, np.float32).reshape(NCH, 128).T)
    d["mukd"] = np.ascontiguousarray(np.asarray(L1_ffn_mu_k, np.float32).reshape(NCH, 128).T)
    d["onesd"] = np.ones((128, 128), dtype=ml_dtypes.bfloat16)
    d["identd"] = np.eye(128, dtype=ml_dtypes.bfloat16)
    return d


def kernel(L0_attn_x, L0_attn_Wz, L0_attn_bz, L0_attn_Wh, L0_attn_bh,
           L0_ffn_w_gate, L0_ffn_w_up, L0_ffn_w_down, L1_attn_W_qkv,
           L1_ffn_W_r, L1_ffn_W_k, L1_ffn_W_v, L1_ffn_mu_r, L1_ffn_mu_k):
    x = np.asarray(L0_attn_x, np.float32)
    B = x.shape[0]
    assert B == 8
    base = make_weight_inputs(L0_attn_Wz, L0_attn_bz, L0_attn_Wh, L0_attn_bh,
                              L0_ffn_w_gate, L0_ffn_w_up, L0_ffn_w_down,
                              L1_attn_W_qkv, L1_ffn_W_r, L1_ffn_W_k, L1_ffn_W_v,
                              L1_ffn_mu_r, L1_ffn_mu_k)
    in_maps = []
    for c in range(B):
        m = dict(base)
        m["xt"] = np.ascontiguousarray(x[c].T)
        in_maps.append(m)
    nc = build_program()
    res = run_bass_kernel_spmd(nc, in_maps, core_ids=list(range(8)))
    outs = [np.ascontiguousarray(np.asarray(r["out"]).T) for r in res.results]
    return np.stack(outs, axis=0).astype(np.float32)
